# version 01
# baseline (speedup 1.0000x reference)
import numpy as np
import ml_dtypes
from contextlib import ExitStack

import concourse.bass as bass
import concourse.mybir as mybir
from concourse.alu_op_type import AluOpType as OP
from concourse.bass_utils import run_bass_kernel_spmd

F32 = mybir.dt.float32
BF16 = mybir.dt.bfloat16
AF = mybir.ActivationFunctionType
AX = mybir.AxisListType

GAMMA = 0.96875
B, T, C = 2, 8192, 1024
TC = 2048
CHUNK = 512
NT = TC // 128
EPS = 1e-5
BF = ml_dtypes.bfloat16

ENGS = ("sync", "gpsimd", "tensor", "vector", "scalar")


class SemT:
    def __init__(self, nc, name):
        self.h = nc.alloc_semaphore(name=name)
        self.n = 0


class Eng:

    def __init__(self, prog, name):
        self.p = prog
        self.name = name
        self.hwm = {}
        self.selfsem = None

    def _append(self, fn):
        self.p.ops[self.name].append(fn)

    def op(self, fn):
        if self.selfsem is not None:
            s = self.selfsem
            self.wait(s, s.n)
            self._append(lambda e, fn=fn, h=s.h: fn(e).then_inc(h, 1))
            s.n += 1
            return s.n
        self._append(fn)
        return None

    def inc(self, fn, sem, k=1):
        if self.selfsem is not None and sem is self.selfsem:
            return self.op(fn)
        if self.selfsem is not None and k == 1:
            self.wait(self.selfsem, self.selfsem.n)
        self._append(lambda e, fn=fn, s=sem.h, k=k: fn(e).then_inc(s, k))
        sem.n += k
        return sem.n

    def dma(self, fn, sem):
        self._append(lambda e, fn=fn, s=sem.h: fn(e).then_inc(s, 16))
        sem.n += 16
        return sem.n

    def wait(self, sem, val):
        if val <= 0:
            return
        key = id(sem)
        if self.hwm.get(key, -1) >= val:
            return
        self.hwm[key] = val
        self._append(lambda e, s=sem.h, v=val: e.wait_ge(s, v))


class Prog:
    def __init__(self):
        self.nc = bass.Bass("TRN2", target_bir_lowering=False)
        self.ops = {e: [] for e in ENGS}
        self.stack = ExitStack()
        self._names = {}
        self.e = {n: Eng(self, n) for n in ENGS}

    def _nm(self, base):
        i = self._names.get(base, 0)
        self._names[base] = i + 1
        return base if i == 0 else f"{base}_{i}"

    def sem(self, name):
        return SemT(self.nc, self._nm(name))

    def sbuf(self, name, shape, dtype):
        return self.stack.enter_context(self.nc.sbuf_tensor(self._nm(name), shape, dtype))

    def psum(self, name, shape, dtype):
        return self.stack.enter_context(self.nc.psum_tensor(self._nm(name), shape, dtype))

    def din(self, name, shape, dtype):
        return self.nc.dram_tensor(name, shape, dtype, kind="ExternalInput").ap()

    def dout(self, name, shape, dtype):
        return self.nc.dram_tensor(name, shape, dtype, kind="ExternalOutput").ap()

    def dtmp(self, name, shape, dtype):
        return self.nc.dram_tensor(self._nm(name), shape, dtype).ap()

    def build(self):
        nc = self.nc
        with self.stack:
            with nc.Block() as block:
                for eng in ENGS:
                    fns = self.ops[eng]

                    def body(e, fns=fns):
                        for f in fns:
                            f(e)

                    getattr(block, eng)(body)
        return nc


def emit_ln_rows(p, lb, sems, n_tiles, src_re, lnxt, lnxt_w, col0,
                 sem_in_issue, extra_consumer_wait=None):
    TE, V, SC, SY = p.e["tensor"], p.e["vector"], p.e["scalar"], p.e["sync"]
    smm, sv, ssc = sems["smm"], sems["sv"], sems["ssc"]
    s_xr = lb["s_xr"]
    stats, mv, xnbf = lb["stats"], lb["mv"], lb["xnbf"]
    idnb = lb["idnb"]
    xr_pool = lb["xr_pool"]
    pt_pool = lb["pt_pool"]
    load_vals = [0] * n_tiles
    free_vals = [0] * n_tiles

    for tt in range(min(2, n_tiles)):
        load_vals[tt] = SY.dma(
            lambda e, tt=tt: e.dma_start(out=xr_pool[tt % 2][:], in_=src_re[:, tt, :]), s_xr[tt % 2])

    for tt in range(n_tiles):
        xr = xr_pool[tt % 2]
        V.wait(s_xr[tt % 2], load_vals[tt])
        V.op(lambda e, xr=xr: e.bn_stats(stats[:, 0:6], xr[:, 0:512]))
        V.op(lambda e, xr=xr: e.bn_stats(stats[:, 6:12], xr[:, 512:1024]))
        v_stats = V.inc(lambda e: e.bn_aggr(mv[:, 0:2], stats[:, 0:12]), sv)
        SC.wait(sv, v_stats)
        SC.wait(lb["sgp"], lb["eps_val"])
        v_sq = SC.inc(lambda e: e.activation(mv[:, 2:3], mv[:, 1:2], AF.Sqrt, bias=lb["eps_sb"][:, 0:1]), ssc)
        V.wait(ssc, v_sq)
        V.op(lambda e: e.reciprocal(mv[:, 2:3], mv[:, 2:3]))
        v_rsig = V.inc(lambda e: e.tensor_scalar(mv[:, 3:4], mv[:, 0:1], mv[:, 2:3], -1.0,
                                                 op0=OP.mult, op1=OP.mult), sv)
        v_app = V.inc(lambda e, xr=xr: e.tensor_scalar(xnbf[:], xr[:], mv[:, 2:3], mv[:, 3:4],
                                                       op0=OP.mult, op1=OP.add), sv)
        free_vals[tt] = v_app
        if tt + 2 < n_tiles:
            SY.wait(sv, v_app)
            load_vals[tt + 2] = SY.dma(
                lambda e, tt=tt: e.dma_start(out=xr_pool[tt % 2][:], in_=src_re[:, tt + 2, :]), s_xr[tt % 2])
        for ct in range(8):
            pt, ptinfo = pt_pool[ct % 2]
            TE.wait(lb["idn_sem"], lb["idn_val"])
            TE.wait(sv, v_app)
            TE.wait(sv, ptinfo["free"])
            v_tr = TE.inc(lambda e, pt=pt, ct=ct: e.transpose(
                pt[:, 0:128], xnbf[:, ct * 128:(ct + 1) * 128], idnb[:]), smm)
            V.wait(smm, v_tr)
            ptinfo["free"] = V.inc(lambda e, pt=pt, ct=ct, tt=tt: e.tensor_copy(
                lnxt[:, ct * lnxt_w + col0 + tt * 128: ct * lnxt_w + col0 + (tt + 1) * 128],
                pt[:, 0:128]), sv)
    return sv.n


def build_A():
    p = Prog()
    TE, V, SC, SY, G = p.e["tensor"], p.e["vector"], p.e["scalar"], p.e["sync"], p.e["gpsimd"]

    XR = p.din("XR", [TC, C], F32)
    WQ = p.din("WQ", [C, 8192], F32)
    WK = p.din("WK", [C, 8192], F32)
    WV = p.din("WV", [C, 8192], F32)
    GP = p.din("GP", [128, 16], F32)
    GPI = p.din("GPI", [128, 16], F32)
    IDN = p.din("IDN", [128, 128], BF16)
    COSR = p.din("COSR", [128, 16 * 64], BF16)
    SINR = p.din("SINR", [128, 16 * 64], BF16)

    OUT_PART = p.dout("OUT_PART", [TC, C], F32)
    S_LOC = p.dout("S_LOC", [8, C, C], BF16)
    KT_SAVE = p.dout("KT_SAVE", [8, C, TC], BF16)
    SSCR = p.dtmp("SSCR", [8, C, C], BF16)

    OUT_re = OUT_PART.rearrange("(a p) c -> p a c", p=128)
    KT_re = KT_SAVE.rearrange("h (a p) t -> p h a t", p=128)
    SL_re = S_LOC.rearrange("h (a p) c -> p h a c", p=128)
    SS_re = SSCR.rearrange("h (a p) c -> p h a c", p=128)
    XR_re = XR.rearrange("(a p) c -> p a c", p=128)

    lnxt = p.sbuf("lnxt", [128, 8 * TC], BF16)
    gp_sb = p.sbuf("gp_sb", [128, 16], F32)
    gpi_sb = p.sbuf("gpi_sb", [128, 16], F32)
    cosr = p.sbuf("cosr", [128, 16 * 64], BF16)
    sinr = p.sbuf("sinr", [128, 16 * 64], BF16)
    idnb = p.sbuf("idnb", [128, 128], BF16)
    lb = {
        "stats": p.sbuf("ln_stats", [128, 12], F32),
        "mv": p.sbuf("ln_mv", [128, 4], F32),
        "xnbf": p.sbuf("ln_xnbf", [128, C], BF16),
        "idnb": idnb,
        "xr_pool": [p.sbuf("xr0", [128, C], F32), p.sbuf("xr1", [128, C], F32)],
    }

    wqb = [p.sbuf("wq0", [128, 8 * 1024], BF16), p.sbuf("wq1", [128, 8 * 1024], BF16)]
    wkb = p.sbuf("wkb", [128, 8 * 1024], BF16)
    wvb = p.sbuf("wvb", [128, 8 * 1024], BF16)
    qp = p.sbuf("qp", [128, 4 * 1024], BF16)
    kp = p.sbuf("kp", [128, 4 * 1024], BF16)
    vp = p.sbuf("vp", [128, 4 * 1024], BF16)
    qt = p.sbuf("qt", [128, 8 * 512], BF16)
    kt = p.sbuf("kt", [128, 8 * 512], BF16)
    msb = p.sbuf("msb", [128, 4 * 512], BF16)
    ssb = p.sbuf("ssb", [128, 8 * 1024], BF16)
    oacc = p.sbuf("oacc", [128, 8 * 1024], F32)
    rope_t = p.sbuf("rope_t", [128, 64], F32)
    rot_t = p.sbuf("rot_t", [128, 64], F32)

    pj_pool = [(p.psum("pj0", [128, 1024], F32), {"free_v": 0, "free_sc": 0}),
               (p.psum("pj1", [128, 1024], F32), {"free_v": 0, "free_sc": 0})]
    po_pool = [(p.psum("po0", [128, 512], F32), {"free": 0}),
               (p.psum("po1", [128, 512], F32), {"free": 0})]
    pt_pool = [(p.psum("ptA", [128, 1024], BF16), {"free": 0}),
               (p.psum("ptB", [128, 1024], BF16), {"free": 0})]
    lb["pt_pool"] = pt_pool

    s_const = p.sem("s_const")
    s_xr = [p.sem("s_xr0"), p.sem("s_xr1")]
    s_wq = [p.sem("s_wq0"), p.sem("s_wq1")]
    s_wkv = p.sem("s_wkv")
    s_sld = p.sem("s_sld")
    s_sst = p.sem("s_sst")
    s_slout = p.sem("s_slout")
    s_kt = p.sem("s_kt")
    s_oacc = p.sem("s_oacc")
    smm = p.sem("smm")
    sv = p.sem("sv")
    ssc = p.sem("ssc")
    sgp = p.sem("sgp")
    sems = {"smm": smm, "sv": sv, "ssc": ssc}
    V.selfsem = sv
    SC.selfsem = ssc
    G.selfsem = sgp
    eps_sb = p.sbuf("eps_sb", [128, 1], F32)
    lb["eps_sb"] = eps_sb
    lb["sgp"] = sgp
    lb["s_xr"] = s_xr
    lb["eps_val"] = G.inc(lambda e: e.memset(eps_sb[:], float(EPS)), sgp)
    lb["idn_sem"] = s_const

    for dst, srcap in ((gp_sb, GP), (gpi_sb, GPI), (idnb, IDN), (cosr, COSR), (sinr, SINR)):
        SY.dma(lambda e, dst=dst, srcap=srcap: e.dma_start(out=dst[:], in_=srcap), s_const)
    consts_val = s_const.n
    lb["idn_val"] = consts_val

    lnxt_done = emit_ln_rows(p, lb, sems, NT, XR_re, lnxt, TC, 0, None)

    w_last_use = [0, 0]
    wkv_last_use = {"v": 0}
    WQre = WQ.rearrange("(a p) o -> p a o", p=128)
    WKre = WK.rearrange("(a p) o -> p a o", p=128)
    WVre = WV.rearrange("(a p) o -> p a o", p=128)

    def issue_wq_load(par, h, after_mm):
        G.wait(smm, after_mm)
        return G.dma(lambda e, par=par, h=h: e.dma_start(
            out=wqb[par].ap().rearrange("p (a o) -> p a o", o=1024),
            in_=WQre[:, :, h * 1024:(h + 1) * 1024]), s_wq[par])

    def issue_wkv_load(h, after_mm):
        G.wait(smm, after_mm)
        for dst, Wre in ((wkb, WKre), (wvb, WVre)):
            G.dma(lambda e, dst=dst, h=h, Wre=Wre: e.dma_start(
                out=dst.ap().rearrange("p (a o) -> p a o", o=1024),
                in_=Wre[:, :, h * 1024:(h + 1) * 1024]), s_wkv)
        return s_wkv.n

    def issue_state_load(h, after_mm):
        G.wait(smm, after_mm)
        G.wait(s_sst, 16 * 8)
        G.wait(s_slout, 16 * h)
        return G.dma(lambda e, h=h: e.dma_start(
            out=ssb.ap().rearrange("p (a c) -> p a c", c=1024),
            in_=SS_re[:, h, :, :]), s_sld)

    wq_vals = {(0, 0): issue_wq_load(0, 0, 0)}
    wkv_vals = {}
    state_store_vals = [0] * 8
    kt_store_val = {"v": 0}
    oacc_store_val = {"v": 0}
    rope_free = {"v": 0}

    pairs = [(3, 2), (1, 0)]
    lnW = TC

    for pi, (jA, jB) in enumerate(pairs):
        pair_base = jB * 4
        for h in range(8):
            par = (pi * 8 + h) % 2
            wqv = wq_vals.pop((pi, h))
            wkvv = issue_wkv_load(h, wkv_last_use["v"])
            st_val = issue_state_load(h, smm.n) if pi == 1 else 0
            state_added = 0

            for ci, j in enumerate((jA, jB)):
                ttg0 = 4 * j
                have_state = not (pi == 0 and ci == 0)
                for m, (dst, has_rope) in enumerate(((qp, True), (kp, True), (vp, False))):
                    wmat = wqb[par] if m == 0 else (wkb if m == 1 else wvb)
                    wsem = s_wq[par] if m == 0 else s_wkv
                    wwait = wqv if m == 0 else wkvv
                    for ti in range(4):
                        ttg = ttg0 + ti
                        pj, pinfo = pj_pool[(m * 4 + ti) % 2]
                        TE.wait(sv, pinfo["free_v"])
                        TE.wait(ssc, pinfo["free_sc"])
                        TE.wait(s_const, consts_val)
                        TE.wait(wsem, wwait)
                        TE.wait(sv, lnxt_done)
                        for ch in range(2):
                            for ct in range(8):
                                last = (ch == 1 and ct == 7)
                                fn = lambda e, pj=pj, ch=ch, ct=ct, ttg=ttg, wmat=wmat: e.matmul(
                                    pj[:, ch * 512:(ch + 1) * 512],
                                    lnxt[:, ct * lnW + ttg * 128: ct * lnW + (ttg + 1) * 128],
                                    wmat[:, ct * 1024 + ch * 512: ct * 1024 + (ch + 1) * 512],
                                    start=(ct == 0), stop=(ct == 7))
                                if last:
                                    v_mm = TE.inc(fn, smm)
                                else:
                                    TE.op(fn)
                        if m == 0:
                            w_last_use[par] = v_mm
                        else:
                            wkv_last_use["v"] = v_mm
                        if has_rope:
                            V.wait(smm, v_mm)
                            V.wait(ssc, rope_free["v"])
                            V.op(lambda e, pj=pj: e.tensor_scalar(
                                rot_t[:, 0:64:2], pj[:, 1:64:2], -1.0, None, op0=OP.mult))
                            V.op(lambda e, pj=pj: e.tensor_copy(rot_t[:, 1:64:2], pj[:, 0:64:2]))
                            V.op(lambda e, pj=pj, ttg=ttg: e.tensor_tensor(
                                rope_t[:], pj[:, 0:64], cosr[:, ttg * 64:(ttg + 1) * 64], op=OP.mult))
                            V.op(lambda e, ttg=ttg: e.tensor_tensor(
                                rot_t[:], rot_t[:], sinr[:, ttg * 64:(ttg + 1) * 64], op=OP.mult))
                            v_rope = V.inc(lambda e: e.tensor_tensor(
                                rope_t[:], rope_t[:], rot_t[:], op=OP.add), sv)
                            SC.wait(smm, v_mm)
                            SC.wait(sv, v_rope)
                            SC.op(lambda e, dst=dst, ti=ti: e.activation(
                                dst[:, ti * 1024: ti * 1024 + 64], rope_t[:], AF.Silu))
                            SC.op(lambda e, pj=pj, dst=dst, ti=ti: e.activation(
                                dst[:, ti * 1024 + 64: ti * 1024 + 512], pj[:, 64:512], AF.Silu))
                            v_sil = SC.inc(lambda e, pj=pj, dst=dst, ti=ti: e.activation(
                                dst[:, ti * 1024 + 512: ti * 1024 + 1024], pj[:, 512:1024], AF.Silu), ssc)
                            pinfo["free_sc"] = v_sil
                            pinfo["free_v"] = v_rope
                            rope_free["v"] = v_sil
                        else:
                            SC.wait(smm, v_mm)
                            SC.op(lambda e, pj=pj, dst=dst, ti=ti: e.activation(
                                dst[:, ti * 1024: ti * 1024 + 512], pj[:, 0:512], AF.Silu))
                            v_sil = SC.inc(lambda e, pj=pj, dst=dst, ti=ti: e.activation(
                                dst[:, ti * 1024 + 512: ti * 1024 + 1024], pj[:, 512:1024], AF.Silu), ssc)
                            pinfo["free_sc"] = v_sil
                proj_done = ssc.n
                if ci == 1:
                    nxt = (pi, h + 1) if h < 7 else ((1, 0) if pi == 0 else None)
                    if nxt is not None:
                        wq_vals[nxt] = issue_wq_load(1 - par, nxt[1], w_last_use[1 - par])
                V.wait(ssc, proj_done)
                for ti in range(4):
                    ttg = ttg0 + ti
                    v_qpg = V.inc(lambda e, ti=ti, ttg=ttg: e.tensor_scalar(
                        vp[:, ti * 1024:(ti + 1) * 1024], vp[:, ti * 1024:(ti + 1) * 1024],
                        gp_sb[:, ttg:ttg + 1], None, op0=OP.mult), sv)
                tr_done = {}
                for src, dsttile, nm in ((qp, qt, "q"), (kp, kt, "k")):
                    if nm == "k":
                        V.wait(s_kt, kt_store_val["v"])
                    for ct in range(8):
                        pt, ptinfo = pt_pool[ct % 2]
                        for ti in range(4):
                            TE.wait(ssc, proj_done)
                            TE.wait(sv, ptinfo["free"])
                            last = (ti == 3)
                            fn = lambda e, pt=pt, src=src, ct=ct, ti=ti: e.transpose(
                                pt[:, ti * 128:(ti + 1) * 128],
                                src[:, ti * 1024 + ct * 128: ti * 1024 + (ct + 1) * 128], idnb[:])
                            if last:
                                v_tr = TE.inc(fn, smm)
                            else:
                                TE.op(fn)
                        V.wait(smm, v_tr)
                        ptinfo["free"] = V.inc(lambda e, pt=pt, dsttile=dsttile, ct=ct: e.tensor_copy(
                            dsttile[:, ct * 512:(ct + 1) * 512], pt[:, 0:512]), sv)
                    tr_done[nm] = sv.n
                SY.wait(sv, tr_done["k"])
                kt_store_val["v"] = SY.dma(lambda e, h=h, j=j: e.dma_start(
                    out=KT_re[:, h, :, j * 512:(j + 1) * 512],
                    in_=kt.ap().rearrange("p (a t) -> p a t", t=512)), s_kt)
                s1_evac = [0] * 4
                s1_mask = [0] * 4
                for ka in range(4):
                    qmax = 128 * (ka + 1)
                    pj, pinfo = pj_pool[ka % 2]
                    TE.wait(sv, max(pinfo["free_v"], tr_done["q"], tr_done["k"]))
                    TE.wait(ssc, pinfo["free_sc"])
                    for ct in range(8):
                        last = (ct == 7)
                        fn = lambda e, pj=pj, ct=ct, ka=ka, qmax=qmax: e.matmul(
                            pj[:, 0:qmax],
                            qt[:, ct * 512 + ka * 128: ct * 512 + (ka + 1) * 128],
                            kt[:, ct * 512: ct * 512 + qmax],
                            start=(ct == 0), stop=(ct == 7))
                        if last:
                            v_mm = TE.inc(fn, smm)
                        else:
                            TE.op(fn)
                    V.wait(smm, v_mm)
                    ev = V.inc(lambda e, pj=pj, ka=ka, qmax=qmax: e.tensor_copy(
                        msb[:, ka * 512: ka * 512 + qmax], pj[:, 0:qmax]), sv)
                    pinfo["free_v"] = ev
                    pinfo["free_sc"] = 0
                    s1_evac[ka] = ev
                    G.wait(sv, ev)
                    s1_mask[ka] = G.inc(lambda e, ka=ka: e.affine_select(
                        msb[:, ka * 512 + ka * 128: ka * 512 + (ka + 1) * 128],
                        msb[:, ka * 512 + ka * 128: ka * 512 + (ka + 1) * 128],
                        pattern=[[-1, 128]], compare_op=OP.is_ge, fill=0.0,
                        base=0, channel_multiplier=1), sgp)
                for b in range(4):
                    ttg = ttg0 + b
                    tp = ttg - pair_base
                    for half in range(2):
                        po, poinfo = po_pool[(b * 2 + half) % 2]
                        TE.wait(sv, poinfo["free"])
                        TE.wait(sgp, s1_mask[b])
                        TE.wait(sv, max(s1_evac))
                        TE.wait(sv, v_qpg)
                        if have_state:
                            if pi == 1:
                                TE.wait(s_sld, st_val)
                            TE.wait(sv, state_added)
                        n_within = 4 - b
                        n_cross = 8 if have_state else 0
                        n_tot = n_within + n_cross
                        fns = []
                        idx = 0
                        for ka in range(b, 4):
                            st, sp = idx == 0, idx == n_tot - 1
                            fns.append(lambda e, po=po, ka=ka, b=b, half=half, st=st, sp=sp: e.matmul(
                                po[:, 0:512],
                                msb[:, ka * 512 + b * 128: ka * 512 + (b + 1) * 128],
                                vp[:, ka * 1024 + half * 512: ka * 1024 + (half + 1) * 512],
                                start=st, stop=sp))
                            idx += 1
                        if have_state:
                            for ct in range(8):
                                st, sp = idx == 0, idx == n_tot - 1
                                fns.append(lambda e, po=po, ct=ct, b=b, half=half, st=st, sp=sp: e.matmul(
                                    po[:, 0:512],
                                    kt[:, ct * 512 + b * 128: ct * 512 + (b + 1) * 128],
                                    ssb[:, ct * 1024 + half * 512: ct * 1024 + (half + 1) * 512],
                                    start=st, stop=sp))
                                idx += 1
                        for fn in fns[:-1]:
                            TE.op(fn)
                        v_mm = TE.inc(fns[-1], smm)
                        V.wait(smm, v_mm)
                        if h == 0:
                            V.wait(s_oacc, oacc_store_val["v"])
                            ev = V.inc(lambda e, po=po, tp=tp, half=half, ttg=ttg: e.tensor_scalar(
                                oacc[:, tp * 1024 + half * 512: tp * 1024 + (half + 1) * 512],
                                po[:, 0:512], gpi_sb[:, ttg:ttg + 1], None, op0=OP.mult), sv)
                        else:
                            ev = V.inc(lambda e, po=po, tp=tp, half=half, ttg=ttg: e.scalar_tensor_tensor(
                                oacc[:, tp * 1024 + half * 512: tp * 1024 + (half + 1) * 512],
                                po[:, 0:512], gpi_sb[:, ttg:ttg + 1],
                                oacc[:, tp * 1024 + half * 512: tp * 1024 + (half + 1) * 512],
                                op0=OP.mult, op1=OP.add), sv)
                        poinfo["free"] = ev
                first_chunk = (pi == 0 and ci == 0)
                for c1t in range(8):
                    for half in range(2):
                        po, poinfo = po_pool[(c1t * 2 + half) % 2]
                        TE.wait(sv, poinfo["free"])
                        TE.wait(sv, v_qpg)
                        if pi == 1:
                            TE.wait(s_sld, st_val)
                        for kti in range(4):
                            last = kti == 3
                            fn = lambda e, po=po, c1t=c1t, half=half, kti=kti: e.matmul(
                                po[:, 0:512],
                                qp[:, kti * 1024 + c1t * 128: kti * 1024 + (c1t + 1) * 128],
                                vp[:, kti * 1024 + half * 512: kti * 1024 + (half + 1) * 512],
                                start=(kti == 0), stop=(kti == 3))
                            if last:
                                v_mm = TE.inc(fn, smm)
                            else:
                                TE.op(fn)
                        V.wait(smm, v_mm)
                        if first_chunk:
                            V.wait(s_sst, 16 * h)
                            ev = V.inc(lambda e, po=po, c1t=c1t, half=half: e.tensor_copy(
                                ssb[:, c1t * 1024 + half * 512: c1t * 1024 + (half + 1) * 512],
                                po[:, 0:512]), sv)
                        else:
                            ev = V.inc(lambda e, po=po, c1t=c1t, half=half: e.tensor_tensor(
                                ssb[:, c1t * 1024 + half * 512: c1t * 1024 + (half + 1) * 512],
                                ssb[:, c1t * 1024 + half * 512: c1t * 1024 + (half + 1) * 512],
                                po[:, 0:512], op=OP.add), sv)
                        poinfo["free"] = ev
                state_added = sv.n
            G.wait(sv, state_added)
            dst = SS_re if pi == 0 else SL_re
            state_store_vals[h] = G.dma(lambda e, h=h, dst=dst: e.dma_start(
                out=dst[:, h, :, :],
                in_=ssb.ap().rearrange("p (a c) -> p a c", c=1024)), s_sst if pi == 0 else s_slout)
        SY.wait(sv, sv.n)
        oacc_store_val["v"] = SY.dma(lambda e, pair_base=pair_base: e.dma_start(
            out=OUT_re[:, pair_base:pair_base + 8, :],
            in_=oacc.ap().rearrange("p (a c) -> p a c", c=1024)), s_oacc)

    SY.wait(s_kt, s_kt.n)
    SY.wait(s_oacc, s_oacc.n)
    SY.wait(s_slout, s_slout.n)
    SY.wait(s_sst, s_sst.n)
    return p.build()


def consts_A(r):
    tl = np.arange(TC)
    lg = np.log(np.float64(GAMMA))
    gp = np.exp(tl * lg).astype(np.float32)
    gpi = np.exp(-tl * lg).astype(np.float32)
    gp2 = gp.reshape(16, 128).T.copy()
    gpi2 = gpi.reshape(16, 128).T.copy()
    idn = np.eye(128, dtype=BF)
    pos = (r * TC + tl).astype(np.float32)
    inv_freq = 1.0 / 10000.0 ** (np.arange(0, 64, 2, dtype=np.float32) / 64)
    freqs = pos[:, None] * inv_freq[None, :]
    freqs = np.repeat(freqs, 2, axis=-1)
    cos = np.cos(freqs).astype(BF).reshape(16, 128, 64).transpose(1, 0, 2).reshape(128, 16 * 64).copy()
    sin = np.sin(freqs).astype(BF).reshape(16, 128, 64).transpose(1, 0, 2).reshape(128, 16 * 64).copy()
    return {"GP": gp2, "GPI": gpi2, "IDN": idn, "COSR": cos, "SINR": sin}


def build_A2():
    p = Prog()
    TE, V, SC, SY, G = p.e["tensor"], p.e["vector"], p.e["scalar"], p.e["sync"], p.e["gpsimd"]

    OUTP = p.din("OUTP", [TC, C], F32)
    S_IN = p.din("S_IN", [8, C, C], BF16)
    KT = p.din("KT", [8, C, TC], BF16)
    XR = p.din("XR", [TC, C], F32)
    GNW = p.din("GNW", [128, C], F32)
    GNB = p.din("GNB", [128, C], F32)
    GCORR = p.din("GCORR", [128, 16], F32)
    X1 = p.dout("X1", [TC, C], F32)

    OUT_re = OUTP.rearrange("(a p) c -> p a c", p=128)
    XR_re = XR.rearrange("(a p) c -> p a c", p=128)
    X1_re = X1.rearrange("(a p) c -> p a c", p=128)
    KT_re = KT.rearrange("h (a p) t -> p h a t", p=128)
    SI_re = S_IN.rearrange("h (a p) c -> p h a c", p=128)

    acc = p.sbuf("acc", [128, 16 * 1024], F32)
    kth = [p.sbuf("kth0", [128, 8 * 2048], BF16), p.sbuf("kth1", [128, 8 * 2048], BF16)]
    sih = [p.sbuf("sih0", [128, 8 * 1024], BF16), p.sbuf("sih1", [128, 8 * 1024], BF16)]
    gnw_sb = p.sbuf("gnw_sb", [128, C], F32)
    gnb_sb = p.sbuf("gnb_sb", [128, C], F32)
    gcorr_sb = p.sbuf("gcorr_sb", [128, 16], F32)
    outt = [p.sbuf("outt0", [128, C], F32), p.sbuf("outt1", [128, C], F32)]
    xrt = [p.sbuf("xrt0", [128, C], F32), p.sbuf("xrt1", [128, C], F32)]
    tmpA = p.sbuf("tmpA", [128, C], F32)
    tmpB = p.sbuf("tmpB", [128, C], F32)
    stats6 = p.sbuf("stats6", [128, 96], F32)
    gmv = p.sbuf("gmv", [128, 32], F32)
    rsig = p.sbuf("rsig", [128, 16], F32)
    negm = p.sbuf("negm", [128, 16], F32)
    eps_sb = p.sbuf("eps_a2", [128, 1], F32)

    pc = [(p.psum("pc0", [128, 1024], F32), {"free": 0}),
          (p.psum("pc1", [128, 1024], F32), {"free": 0})]

    s_const = p.sem("s_const")
    s_kt = [p.sem("s_kt0"), p.sem("s_kt1")]
    s_si = [p.sem("s_si0"), p.sem("s_si1")]
    s_ot = [p.sem("s_ot0"), p.sem("s_ot1")]
    s_xt = [p.sem("s_xt0"), p.sem("s_xt1")]
    s_x1 = p.sem("s_x1")
    smm = p.sem("smm")
    sv = p.sem("sv")
    ssc = p.sem("ssc")
    sgp = p.sem("sgp")
    V.selfsem = sv
    SC.selfsem = ssc
    G.selfsem = sgp

    eps_val = G.inc(lambda e: e.memset(eps_sb[:], float(EPS)), sgp)
    for dst, srcap in ((gnw_sb, GNW), (gnb_sb, GNB), (gcorr_sb, GCORR)):
        SY.dma(lambda e, dst=dst, srcap=srcap: e.dma_start(out=dst[:], in_=srcap), s_const)
    consts_val = s_const.n

    kt_last_use = [0, 0]
    si_last_use = [0, 0]

    def load_head(h):
        sl = h % 2
        G.wait(smm, kt_last_use[sl])
        kv = G.dma(lambda e, h=h, sl=sl: e.dma_start(
            out=kth[sl].ap().rearrange("p (a t) -> p a t", t=2048), in_=KT_re[:, h, :, :]), s_kt[sl])
        G.wait(smm, si_last_use[sl])
        sval = G.dma(lambda e, h=h, sl=sl: e.dma_start(
            out=sih[sl].ap().rearrange("p (a c) -> p a c", c=1024), in_=SI_re[:, h, :, :]), s_si[sl])
        return kv, sval

    hvals = {0: load_head(0)}
    for h in range(8):
        sl = h % 2
        kv, sval = hvals.pop(h)
        if h < 7:
            hvals[h + 1] = load_head(h + 1)
        for qt in range(16):
            po, poinfo = pc[qt % 2]
            TE.wait(sv, poinfo["free"])
            TE.wait(s_kt[sl], kv)
            TE.wait(s_si[sl], sval)
            fns = []
            for ct1 in range(8):
                for half in range(2):
                    st = (ct1 == 0)
                    sp = (ct1 == 7)
                    fns.append(lambda e, po=po, ct1=ct1, half=half, st=st, sp=sp, sl=sl, qt=qt: e.matmul(
                        po[:, half * 512:(half + 1) * 512],
                        kth[sl][:, ct1 * 2048 + qt * 128: ct1 * 2048 + (qt + 1) * 128],
                        sih[sl][:, ct1 * 1024 + half * 512: ct1 * 1024 + (half + 1) * 512],
                        start=st, stop=sp))
            for fn in fns[:-1]:
                TE.op(fn)
            v_mm = TE.inc(fns[-1], smm)
            kt_last_use[sl] = v_mm
            si_last_use[sl] = v_mm
            V.wait(smm, v_mm)
            if h == 0:
                poinfo["free"] = V.op(lambda e, po=po, qt=qt: e.tensor_copy(
                    acc[:, qt * 1024:(qt + 1) * 1024], po[:]))
            else:
                poinfo["free"] = V.op(lambda e, po=po, qt=qt: e.tensor_tensor(
                    acc[:, qt * 1024:(qt + 1) * 1024], acc[:, qt * 1024:(qt + 1) * 1024],
                    po[:], op=OP.add))
    acc_done = sv.n

    ot_vals = [0] * 16
    xt_vals = [0] * 16
    for tt in range(min(2, 16)):
        ot_vals[tt] = SY.dma(lambda e, tt=tt: e.dma_start(out=outt[tt % 2][:], in_=OUT_re[:, tt, :]), s_ot[tt % 2])
        xt_vals[tt] = SY.dma(lambda e, tt=tt: e.dma_start(out=xrt[tt % 2][:], in_=XR_re[:, tt, :]), s_xt[tt % 2])
    for tt in range(16):
        sl = tt % 2
        V.wait(s_ot[sl], ot_vals[tt])
        V.wait(s_const, consts_val)
        V.wait(sv, acc_done)
        V.op(lambda e, tt=tt, sl=sl: e.scalar_tensor_tensor(
            tmpA[:], acc[:, tt * 1024:(tt + 1) * 1024], gcorr_sb[:, tt:tt + 1], outt[sl][:],
            op0=OP.mult, op1=OP.add))
        for g in range(16):
            V.op(lambda e, g=g: e.bn_stats(stats6[:, 6 * g:6 * g + 6], tmpA[:, g * 64:(g + 1) * 64]))
        for g in range(16):
            V.op(lambda e, g=g: e.bn_aggr(gmv[:, 2 * g:2 * g + 2], stats6[:, 6 * g:6 * g + 6]))
        v_gmv = sv.n
        SC.wait(sv, v_gmv)
        SC.wait(sgp, eps_val)
        v_sq = SC.inc(lambda e: e.activation(rsig[:], gmv[:, 1:32:2], AF.Sqrt, bias=eps_sb[:, 0:1]), ssc)
        V.wait(ssc, v_sq)
        V.op(lambda e: e.reciprocal(rsig[:], rsig[:]))
        V.op(lambda e: e.tensor_tensor(negm[:], gmv[:, 0:32:2], rsig[:], op=OP.mult))
        V.op(lambda e: e.tensor_scalar(negm[:], negm[:], -1.0, None, op0=OP.mult))
        V.wait(s_x1, 16 * tt)
        for g in range(16):
            V.op(lambda e, g=g: e.tensor_scalar(
                tmpB[:, g * 64:(g + 1) * 64], tmpA[:, g * 64:(g + 1) * 64],
                rsig[:, g:g + 1], negm[:, g:g + 1], op0=OP.mult, op1=OP.add))
        V.op(lambda e: e.tensor_tensor(tmpB[:], tmpB[:], gnw_sb[:], op=OP.mult))
        V.op(lambda e: e.tensor_tensor(tmpB[:], tmpB[:], gnb_sb[:], op=OP.add))
        V.wait(s_xt[sl], xt_vals[tt])
        v_fin = V.op(lambda e, sl=sl: e.tensor_tensor(tmpB[:], tmpB[:], xrt[sl][:], op=OP.add))
        SY.wait(sv, v_fin)
        SY.dma(lambda e, tt=tt: e.dma_start(out=X1_re[:, tt, :], in_=tmpB[:]), s_x1)
        if tt + 2 < 16:
            ot_vals[tt + 2] = SY.dma(lambda e, tt=tt, sl=sl: e.dma_start(
                out=outt[sl][:], in_=OUT_re[:, tt + 2, :]), s_ot[sl])
            xt_vals[tt + 2] = SY.dma(lambda e, tt=tt, sl=sl: e.dma_start(
                out=xrt[sl][:], in_=XR_re[:, tt + 2, :]), s_xt[sl])
    SY.wait(s_x1, s_x1.n)
    return p.build()


HALO_B = 1152
NTB = 25
YW = 3200


def build_B(pre_ln, final_res, debug=False):
    p = Prog()
    TE, V, SC, SY, G = p.e["tensor"], p.e["vector"], p.e["scalar"], p.e["sync"], p.e["gpsimd"]

    YH = p.din("YH", [HALO_B + TC, C], F32)
    WQKV = p.din("WQKV", [C, 1536], F32)
    WO = p.din("WO", [512, C], F32)
    W1 = p.din("W1", [C, 5460], F32)
    W2F = p.din("W2F", [2688, C], F32)
    W2L = p.din("W2L", [42, C], F32)
    B1A = p.din("B1A", [128, 22], F32)
    B1G = p.din("B1G", [128, 22], F32)
    B2R = p.din("B2R", [128, C], F32)
    LGC = p.din("LGC", [128, 8], F32)
    LBC = p.din("LBC", [128, 8], F32)
    QSR = p.din("QSR", [128, 512], F32)
    KSR = p.din("KSR", [128, 512], F32)
    CSQ = p.din("CSQ", [128, 24 * 512], BF16)
    SSQ = p.din("SSQ", [128, 24 * 512], BF16)
    CSK = p.din("CSK", [128, 24 * 512], BF16)
    SSK = p.din("SSK", [128, 24 * 512], BF16)
    MASKS = p.din("MASKS", [6, 512, 1024], BF16)
    IDN = p.din("IDN", [128, 128], BF16)
    if final_res:
        X1R = p.din("X1R", [TC, C], F32)
        X1_re = X1R.rearrange("(a p) c -> p a c", p=128)
    YOUT = p.dout("YOUT", [TC, C], F32)
    YOUT_re = YOUT.rearrange("(a p) c -> p a c", p=128)
    YH_re = YH.rearrange("(a p) c -> p a c", p=128)

    YMID = p.dtmp("YMID", [512 + TC, C], F32)
    YMID_re = YMID.rearrange("(a p) c -> p a c", p=128)
    if pre_ln:
        YSRC = p.dtmp("YSRC", [HALO_B + TC, C], F32)
        YSRC_re = YSRC.rearrange("(a p) c -> p a c", p=128)
    else:
        YSRC_re = YH_re

    arena = p.sbuf("arena", [128, 33792], BF16)
    ytb = arena[:, 0:25600]
    w2_sb = arena[:, 0:22528]
    hT_sb = arena[:, 22528:33792]
    wqkv_sb = p.sbuf("wqkv_sb", [128, 8 * 1536], BF16)
    wo_sb = arena[:, 25600:29696]
    idnb = p.sbuf("idnb", [128, 128], BF16)
    qsr_sb = p.sbuf("qsr_sb", [128, 512], F32)
    ksr_sb = p.sbuf("ksr_sb", [128, 512], F32)
    msk_sb = [p.sbuf("msk0", [128, 1024], BF16), p.sbuf("msk1", [128, 1024], BF16)]
    qblk = p.sbuf("qblk", [128, 4 * 512], BF16)
    qTb = p.sbuf("qTb", [128, 4 * 512], BF16)
    kblk = [p.sbuf("kb0", [128, 4 * 512], BF16), p.sbuf("kb1", [128, 4 * 512], BF16)]
    kTb = [p.sbuf("kT0", [128, 4 * 512], BF16), p.sbuf("kT1", [128, 4 * 512], BF16)]
    vblk = [p.sbuf("vb0", [128, 4 * 512], BF16), p.sbuf("vb1", [128, 4 * 512], BF16)]
    rq = [p.sbuf("rq0", [128, 4 * 512], BF16), p.sbuf("rq1", [128, 4 * 512], BF16)]
    rq2 = [p.sbuf("rq20", [128, 4 * 512], BF16), p.sbuf("rq21", [128, 4 * 512], BF16)]
    rk = [p.sbuf("rk0", [128, 4 * 512], BF16), p.sbuf("rk1", [128, 4 * 512], BF16)]
    rk2 = [p.sbuf("rk20", [128, 4 * 512], BF16), p.sbuf("rk21", [128, 4 * 512], BF16)]
    scrf = p.sbuf("scrf", [128, 1536], F32)
    qn = scrf[:, 0:512]
    rot = scrf[:, 512:1024]
    t1s = scrf[:, 1024:1536]
    ssq_s = p.sbuf("ssq_s", [128, 8], F32)
    nrm = p.sbuf("nrm", [128, 8], F32)
    scrg = p.sbuf("scrg", [128, 2048], F32)
    simsb = scrg[:, 0:1024]
    nmx = p.sbuf("nmx", [128, 1], F32)
    zsum = p.sbuf("zsum", [128, 1], F32)
    esb = p.sbuf("esb", [128, 1024], BF16)
    eT = p.sbuf("eT", [128, 8 * 128], BF16)
    osb = p.sbuf("osb", [128, 4 * 128], BF16)
    yt_f32 = [p.sbuf("ytf0", [128, C], F32), p.sbuf("ytf1", [128, C], F32)]
    ysr_t = [p.sbuf("ysr0", [128, C], F32), p.sbuf("ysr1", [128, C], F32)]
    ymid_t = p.sbuf("ymid_t", [128, C], F32)
    lnb_stats = p.sbuf("lnb_stats", [128, 12], F32)
    lnb_mv = p.sbuf("lnb_mv", [128, 4], F32)
    lnb_x = p.sbuf("lnb_x", [128, C], BF16)
    xt_sb = wqkv_sb[:, 0:4096]
    w1a = [wqkv_sb[:, 4096:5120], wqkv_sb[:, 5120:6144]]
    w1g = [wqkv_sb[:, 6144:7168], wqkv_sb[:, 7168:8192]]
    asb = scrf[:, 1024:1536]
    b1a_sb = p.sbuf("b1a_sb", [128, 22], F32)
    b1g_sb = p.sbuf("b1g_sb", [128, 22], F32)
    b2r_sb = scrf[:, 0:1024]
    lgc_sb = p.sbuf("lgc_sb", [128, 8], F32)
    lbc_sb = p.sbuf("lbc_sb", [128, 8], F32)
    h2sb = yt_f32
    x1t = [scrg[:, 0:1024], scrg[:, 1024:2048]]
    eps_sb = p.sbuf("eps_b", [128, 1], F32)
    yo_t = ymid_t

    pj_pool = [(p.psum("pjB0", [128, 512], F32), {"free": 0, "free_sc": 0}),
               (p.psum("pjB1", [128, 512], F32), {"free": 0, "free_sc": 0})]
    ps_pool = [(p.psum("psB0", [128, 1024], F32), {"free": 0}),
               (p.psum("psB1", [128, 1024], F32), {"free": 0})]
    pt_pool = [(p.psum("ptB0", [128, 1024], BF16), {"free": 0}),
               (p.psum("ptB1", [128, 1024], BF16), {"free": 0})]

    s_const = p.sem("s_const")
    s_constG = p.sem("s_constG")
    s_y = [p.sem("s_y0"), p.sem("s_y1")]
    s_rq = [p.sem("s_rq0"), p.sem("s_rq1")]
    s_msk = [p.sem("s_msk0"), p.sem("s_msk1")]
    s_ysc = p.sem("s_ysc")
    s_ysr = [p.sem("s_ysr0"), p.sem("s_ysr1")]
    s_ym = p.sem("s_ym")
    s_h2 = [p.sem("s_h20"), p.sem("s_h21")]
    s_w1 = [p.sem("s_w10"), p.sem("s_w11")]
    s_w2 = p.sem("s_w2")
    s_x1t = [p.sem("s_x1t0"), p.sem("s_x1t1")]
    s_yo = p.sem("s_yo")
    smm = p.sem("smm")
    sv = p.sem("sv")
    ssc = p.sem("ssc")
    sgp = p.sem("sgp")
    V.selfsem = sv
    SC.selfsem = ssc
    G.selfsem = sgp

    eps_val = G.inc(lambda e: e.memset(eps_sb[:], float(EPS)), sgp)

    WQKVre = WQKV.rearrange("(a p) o -> p a o", p=128)
    WOre = WO.rearrange("(a p) o -> p a o", p=128)
    G.dma(lambda e: e.dma_start(out=wqkv_sb.ap().rearrange("p (a o) -> p a o", o=1536),
                                in_=WQKVre), s_constG)
    G.dma(lambda e: e.dma_start(out=wo_sb.rearrange("p (a o) -> p a o", o=1024),
                                in_=WOre), s_constG)
    for dst, srcap in ((idnb, IDN), (qsr_sb, QSR), (ksr_sb, KSR), (b1a_sb, B1A),
                       (b1g_sb, B1G), (lgc_sb, LGC), (lbc_sb, LBC)):
        SY.dma(lambda e, dst=dst, srcap=srcap: e.dma_start(out=dst[:], in_=srcap), s_const)
    consts_val = s_const.n

    y_vals = [0] * NTB
    for tt in range(2):
        y_vals[tt] = SY.dma(lambda e, tt=tt: e.dma_start(
            out=yt_f32[tt % 2][:], in_=YH_re[:, tt, :]), s_y[tt % 2])
    for tt in range(NTB):
        sl = tt % 2
        xr = yt_f32[sl]
        V.wait(s_y[sl], y_vals[tt])
        if pre_ln:
            V.op(lambda e, xr=xr: e.bn_stats(lnb_stats[:, 0:6], xr[:, 0:512]))
            V.op(lambda e, xr=xr: e.bn_stats(lnb_stats[:, 6:12], xr[:, 512:1024]))
            v_stats = V.op(lambda e: e.bn_aggr(lnb_mv[:, 0:2], lnb_stats[:, 0:12]))
            SC.wait(sv, v_stats)
            SC.wait(sgp, eps_val)
            v_sq = SC.inc(lambda e: e.activation(lnb_mv[:, 2:3], lnb_mv[:, 1:2], AF.Sqrt,
                                                 bias=eps_sb[:, 0:1]), ssc)
            V.wait(ssc, v_sq)
            V.op(lambda e: e.reciprocal(lnb_mv[:, 2:3], lnb_mv[:, 2:3]))
            V.op(lambda e: e.tensor_scalar(lnb_mv[:, 3:4], lnb_mv[:, 0:1], lnb_mv[:, 2:3], -1.0,
                                           op0=OP.mult, op1=OP.mult))
            V.wait(s_ysc, s_ysc.n)
            v_f32 = V.op(lambda e, xr=xr: e.tensor_scalar(ymid_t[:], xr[:], lnb_mv[:, 2:3],
                                                          lnb_mv[:, 3:4], op0=OP.mult, op1=OP.add))
            v_app = V.op(lambda e: e.tensor_copy(lnb_x[:], ymid_t[:]))
            SY.wait(sv, v_f32)
            SY.dma(lambda e, tt=tt: e.dma_start(out=YSRC_re[:, tt, :], in_=ymid_t[:]), s_ysc)
        else:
            v_app = V.op(lambda e, xr=xr: e.tensor_copy(lnb_x[:], xr[:]))
        if tt + 2 < NTB:
            SY.wait(sv, v_app)
            y_vals[tt + 2] = SY.dma(lambda e, tt=tt, sl=sl: e.dma_start(
                out=yt_f32[sl][:], in_=YH_re[:, tt + 2, :]), s_y[sl])
        for ct in range(8):
            pt, ptinfo = pt_pool[ct % 2]
            TE.wait(s_const, consts_val)
            TE.wait(sv, v_app)
            TE.wait(sv, ptinfo["free"])
            v_tr = TE.inc(lambda e, pt=pt, ct=ct: e.transpose(
                pt[:, 0:128], lnb_x[:, ct * 128:(ct + 1) * 128], idnb[:]), smm)
            V.wait(smm, v_tr)
            ptinfo["free"] = V.op(lambda e, pt=pt, ct=ct, tt=tt: e.tensor_copy(
                ytb[:, ct * YW + tt * 128: ct * YW + (tt + 1) * 128], pt[:, 0:128]))
    ytb_done = sv.n
    ysc_done = s_ysc.n

    msk_free = [0, 0]
    rot_last_use = [0, 0]
    kv_last_use = [0, 0]
    q_last_use = 0
    eT_last_use = 0
    esb_last_use = 0
    osb_last_use = 0
    ymid_store = 0

    def issue_block_loads(beta, guard_v):
        sl = beta % 2
        G.wait(sv, guard_v)
        for dst, srcap in ((rq[sl], CSQ), (rq2[sl], SSQ), (rk[sl], CSK), (rk2[sl], SSK)):
            G.dma(lambda e, dst=dst, srcap=srcap, beta=beta: e.dma_start(
                out=dst.ap().rearrange("p (a x) -> p a x", x=512),
                in_=srcap.rearrange("p (a x) -> p a x", x=512)[:, 4 * beta: 4 * beta + 4, :]), s_rq[sl])
        return s_rq[sl].n

    blk_loads = {0: issue_block_loads(0, 0)}

    for beta in range(6):
        sl = beta % 2
        slp = (beta - 1) % 2
        ld = blk_loads.pop(beta)
        if beta < 5:
            blk_loads[beta + 1] = issue_block_loads(beta + 1, rot_last_use[1 - sl])
        tau0 = 128 + 512 * beta
        gt0 = tau0 // 128 - 1
        for ti in range(4):
            for oseg in range(3):
                pj, pinfo = pj_pool[(ti * 3 + oseg) % 2]
                TE.wait(sv, pinfo["free"])
                TE.wait(ssc, pinfo["free_sc"])
                TE.wait(sv, ytb_done)
                TE.wait(s_const, consts_val)
                TE.wait(s_constG, 32)
                for ct in range(8):
                    off = 0 if ct < 4 else -1
                    last = ct == 7
                    fn = lambda e, pj=pj, ct=ct, oseg=oseg, tau0=tau0, ti=ti, off=off: e.matmul(
                        pj[:],
                        ytb[:, ct * YW + tau0 + ti * 128 + off: ct * YW + tau0 + (ti + 1) * 128 + off],
                        wqkv_sb[:, ct * 1536 + oseg * 512: ct * 1536 + (oseg + 1) * 512],
                        start=(ct == 0), stop=(ct == 7))
                    if last:
                        v_mm = TE.inc(fn, smm)
                    else:
                        TE.op(fn)
                if oseg == 2:
                    V.wait(smm, v_mm)
                    V.wait(smm, kv_last_use[sl])
                    pinfo["free"] = V.op(lambda e, pj=pj, ti=ti, sl=sl: e.tensor_copy(
                        vblk[sl][:, ti * 512:(ti + 1) * 512], pj[:]))
                    pinfo["free_sc"] = 0
                else:
                    is_q = oseg == 0
                    r1 = rq[sl] if is_q else rk[sl]
                    r2 = rq2[sl] if is_q else rk2[sl]
                    scl = qsr_sb if is_q else ksr_sb
                    SC.wait(smm, v_mm)
                    SC.wait(sv, sv.n)
                    for s in range(8):
                        SC.op(lambda e, pj=pj, s=s: e.activation(
                            qn[:, s * 64:(s + 1) * 64], pj[:, s * 64:(s + 1) * 64],
                            AF.Square, accum_out=ssq_s[:, s:s + 1]))
                    v_ssq = SC.inc(lambda e: e.activation(nrm[:], ssq_s[:], AF.Sqrt), ssc)
                    pinfo["free_sc"] = v_ssq
                    V.wait(ssc, v_ssq)
                    V.op(lambda e: e.tensor_scalar(nrm[:], nrm[:], 1e-12, None, op0=OP.max))
                    V.op(lambda e: e.reciprocal(nrm[:], nrm[:]))
                    for s in range(8):
                        V.op(lambda e, pj=pj, s=s: e.tensor_scalar(
                            qn[:, s * 64:(s + 1) * 64], pj[:, s * 64:(s + 1) * 64],
                            nrm[:, s:s + 1], None, op0=OP.mult))
                    v_qn = V.op(lambda e, scl=scl: e.tensor_tensor(qn[:], qn[:], scl[:], op=OP.mult))
                    pinfo["free"] = v_qn
                    V.op(lambda e: e.tensor_scalar(
                        rot.rearrange("p (s h) -> p s h", h=64)[:, :, 0:32],
                        qn.rearrange("p (s h) -> p s h", h=64)[:, :, 32:64],
                        -1.0, None, op0=OP.mult))
                    V.op(lambda e: e.tensor_copy(
                        rot.rearrange("p (s h) -> p s h", h=64)[:, :, 32:64],
                        qn.rearrange("p (s h) -> p s h", h=64)[:, :, 0:32]))
                    V.wait(s_rq[sl], ld)
                    gtt = gt0 + ti
                    V.op(lambda e, r1=r1, ti=ti: e.tensor_tensor(
                        t1s[:], qn[:], r1[:, ti * 512:(ti + 1) * 512], op=OP.mult))
                    V.op(lambda e, r2=r2, ti=ti: e.tensor_tensor(
                        rot[:], rot[:], r2[:, ti * 512:(ti + 1) * 512], op=OP.mult))
                    if is_q:
                        V.wait(smm, q_last_use)
                        v_fin = V.op(lambda e, ti=ti: e.tensor_tensor(
                            qblk[:, ti * 512:(ti + 1) * 512], t1s[:], rot[:], op=OP.add))
                    else:
                        V.wait(smm, kv_last_use[sl])
                        v_fin = V.op(lambda e, ti=ti, sl=sl: e.tensor_tensor(
                            kblk[sl][:, ti * 512:(ti + 1) * 512], t1s[:], rot[:], op=OP.add))
                    rot_last_use[sl] = v_fin
        qk_done = sv.n
        qk_done_sc = ssc.n
        v_tr = 0
        for src, dst in ((qblk, qTb), (kblk[sl], kTb[sl])):
            for ti in range(4):
                pt, ptinfo = pt_pool[ti % 2]
                TE.wait(sv, qk_done)
                TE.wait(sv, ptinfo["free"])
                for ct in range(4):
                    last = ct == 3
                    fn = lambda e, pt=pt, src=src, ti=ti, ct=ct: e.transpose(
                        pt[:, ct * 128:(ct + 1) * 128],
                        src[:, ti * 512 + ct * 128: ti * 512 + (ct + 1) * 128], idnb[:])
                    if last:
                        v_tr = TE.inc(fn, smm)
                    else:
                        TE.op(fn)
                V.wait(smm, v_tr)
                ptinfo["free"] = V.op(lambda e, pt=pt, dst=dst, ti=ti: e.tensor_copy(
                    dst.ap().rearrange("p (a x) -> p a x", x=512)[:, :, ti * 128:(ti + 1) * 128],
                    pt.ap().rearrange("p (a x) -> p a x", x=128)[:, 0:4, :]))
        q_last_use = v_tr
        trq_done = sv.n

        if beta >= 1:
            att_mm_last = 0
            for ti in range(4):
                slm = ti % 2
                SY.wait(sv, msk_free[slm])
                mv = SY.dma(lambda e, beta=beta, ti=ti, slm=slm: e.dma_start(
                    out=msk_sb[slm][:],
                    in_=MASKS.rearrange("b (a p) j -> p b a j", p=128)[:, beta, ti, :]), s_msk[slm])
                for g in range(4):
                    po, poinfo = pj_pool[g % 2]
                    for hh in range(2):
                        h = 2 * g + hh
                        hct, hof = h // 2, (h % 2) * 64
                        ps, psinfo = ps_pool[h % 2]
                        TE.wait(sv, psinfo["free"])
                        TE.wait(sv, trq_done)
                        TE.op(lambda e, ps=ps, hct=hct, hof=hof, ti=ti, slp=slp: e.matmul(
                            ps[hof:hof + 64, 0:512] if False else ps[:, 0:512],
                            qTb[hof:hof + 64, hct * 512 + ti * 128: hct * 512 + (ti + 1) * 128],
                            kTb[slp][hof:hof + 64, hct * 512: (hct + 1) * 512],
                            start=True, stop=True))
                        v_mm = TE.inc(lambda e, ps=ps, hct=hct, hof=hof, ti=ti, sl=sl: e.matmul(
                            ps[:, 512:1024],
                            qTb[hof:hof + 64, hct * 512 + ti * 128: hct * 512 + (ti + 1) * 128],
                            kTb[sl][hof:hof + 64, hct * 512: (hct + 1) * 512],
                            start=True, stop=True), smm)
                        att_mm_last = v_mm
                        V.wait(smm, v_mm)
                        V.wait(ssc, qk_done_sc)
                        V.wait(ssc, ssc.n)
                        V.wait(s_rq[sl], ld)
                        V.wait(s_msk[slm], mv)
                        v_msku = V.op(lambda e, ps=ps, slm=slm: e.scalar_tensor_tensor(
                            simsb[:], ps[:], 1.0, msk_sb[slm][:],
                            op0=OP.bypass, op1=OP.add))
                        msk_free[slm] = v_msku
                        v_nm = V.op(lambda e: e.reduce_max(nmx[:], simsb[:], axis=AX.X, negate=True))
                        psinfo["free"] = v_nm
                        SC.wait(sv, v_nm)
                        SC.wait(smm, esb_last_use)
                        v_exp = SC.inc(lambda e: e.activation(esb[:], simsb[:], AF.Exp,
                                                              bias=nmx[:, 0:1], accum_out=zsum[:, 0:1]), ssc)
                        V.wait(ssc, v_exp)
                        V.op(lambda e: e.reciprocal(zsum[:], zsum[:]))
                        v_en = V.op(lambda e: e.tensor_scalar(esb[:], esb[:], zsum[:, 0:1], None,
                                                              op0=OP.mult))
                        for half4 in range(2):
                            pt, ptinfo = pt_pool[half4 % 2]
                            TE.wait(sv, v_en)
                            TE.wait(sv, ptinfo["free"])
                            TE.wait(smm, eT_last_use)
                            for ct in range(4):
                                last = ct == 3
                                fn = lambda e, pt=pt, half4=half4, ct=ct: e.transpose(
                                    pt[:, ct * 128:(ct + 1) * 128],
                                    esb[:, half4 * 512 + ct * 128: half4 * 512 + (ct + 1) * 128],
                                    idnb[:])
                                if last:
                                    v_tr2 = TE.inc(fn, smm)
                                else:
                                    TE.op(fn)
                            esb_last_use = v_tr2
                            V.wait(smm, v_tr2)
                            ptinfo["free"] = V.op(lambda e, pt=pt, half4=half4: e.tensor_copy(
                                eT[:, half4 * 512:(half4 + 1) * 512], pt[:, 0:512]))
                        v_eT = sv.n
                        TE.wait(sv, v_eT)
                        if hh == 0:
                            TE.wait(sv, poinfo["free"])
                        for jt in range(8):
                            vsrc = vblk[slp] if jt < 4 else vblk[sl]
                            jc = (jt % 4) * 512
                            last = jt == 7
                            fn = lambda e, po=po, vsrc=vsrc, jc=jc, jt=jt, h=h, hof=hof: e.matmul(
                                po[hof:hof + 64, 0:128],
                                vsrc[:, jc + h * 64: jc + (h + 1) * 64],
                                eT[:, jt * 128:(jt + 1) * 128],
                                start=(jt == 0), stop=(jt == 7))
                            if last:
                                v_mm2 = TE.inc(fn, smm)
                            else:
                                TE.op(fn)
                        eT_last_use = v_mm2
                        att_mm_last = v_mm2
                    V.wait(smm, v_mm2)
                    V.wait(smm, osb_last_use)
                    poinfo["free"] = V.op(lambda e, po=po, g=g: e.tensor_copy(
                        osb[:, g * 128:(g + 1) * 128], po[:, 0:128]))
                v_osb = sv.n
                ps, psinfo = ps_pool[ti % 2]
                TE.wait(sv, psinfo["free"])
                TE.wait(sv, v_osb)
                idx = 0
                for g4 in range(4):
                    for half in range(2):
                        last = (g4 == 3 and half == 1)
                        fn = lambda e, ps=ps, g4=g4, half=half: e.matmul(
                            ps[:, half * 512:(half + 1) * 512],
                            osb[:, g4 * 128:(g4 + 1) * 128],
                            wo_sb[:, g4 * 1024 + half * 512: g4 * 1024 + (half + 1) * 512],
                            start=(g4 == 0), stop=(g4 == 3))
                        if last:
                            v_wo = TE.inc(fn, smm)
                        else:
                            TE.op(fn)
                osb_last_use = v_wo
                tY = 1 + 4 * beta + ti
                slr = ti % 2
                SY.wait(s_ysc, ysc_done)
                SY.wait(sv, sv.n)
                yv = SY.dma(lambda e, tY=tY, slr=slr: e.dma_start(
                    out=ysr_t[slr][:], in_=YSRC_re[:, tY, :]), s_ysr[slr])
                V.wait(smm, v_wo)
                V.wait(s_ysr[slr], yv)
                V.wait(s_ym, ymid_store)
                v_ym = V.op(lambda e, ps=ps, slr=slr: e.scalar_tensor_tensor(
                    ymid_t[:], ps[:], 1.0, ysr_t[slr][:], op0=OP.bypass, op1=OP.add))
                psinfo["free"] = v_ym
                SY.wait(sv, v_ym)
                tM = 4 * (beta - 1) + ti
                ymid_store = SY.dma(lambda e, tM=tM: e.dma_start(
                    out=YMID_re[:, tM, :], in_=ymid_t[:]), s_ym)
            kv_last_use[slp] = att_mm_last

    ymid_all = s_ym.n

    s_b2r = p.sem("s_b2r")
    SY.wait(sv, sv.n)
    SY.wait(s_ym, ymid_all)
    b2r_val = SY.dma(lambda e: e.dma_start(out=b2r_sb[:, :], in_=B2R), s_b2r)
    W2Fre = W2F.rearrange("(a p) o -> p a o", p=128)
    G.wait(sv, sv.n)
    G.wait(smm, smm.n)
    G.dma(lambda e: e.dma_start(
        out=w2_sb.rearrange("p (a o) -> p a o", o=1024)[:, 0:21, :], in_=W2Fre), s_w2)
    G.dma(lambda e: e.dma_start(
        out=arena[0:42, 21 * 1024: 22 * 1024], in_=W2L), s_w2)
    w2_val = s_w2.n

    w1_last_use = [0, 0]

    def issue_w1_load(pb, guard_mm):
        slw = pb % 2
        wwid = 42 if pb == 21 else 128
        G.wait(smm, guard_mm)
        W1re = W1.rearrange("(a p) o -> p a o", p=128)
        G.dma(lambda e, pb=pb, slw=slw, wwid=wwid: e.dma_start(
            out=w1a[slw].rearrange("p (a o) -> p a o", o=128)[:, :, 0:wwid],
            in_=W1re[:, :, pb * 128: pb * 128 + wwid]), s_w1[slw])
        G.dma(lambda e, pb=pb, slw=slw, wwid=wwid: e.dma_start(
            out=w1g[slw].rearrange("p (a o) -> p a o", o=128)[:, :, 0:wwid],
            in_=W1re[:, :, 2730 + pb * 128: 2730 + pb * 128 + wwid]), s_w1[slw])
        return s_w1[slw].n

    h2_vals = [0, 0]
    x1_vals = [0, 0]
    yo_store = 0
    hT_mm_last = 0

    for qtr in range(4):
        for ti in range(4):
            tloc = qtr * 4 + ti
            slh = ti % 2
            tM = 4 + tloc
            G.wait(s_ym, ymid_all)
            G.wait(sv, h2_vals[slh])
            hv = G.dma(lambda e, tM=tM, slh=slh: e.dma_start(
                out=h2sb[slh][:, 0:512], in_=YMID_re[:, tM, 0:512]), s_h2[slh])
            hv = G.dma(lambda e, tM=tM, slh=slh: e.dma_start(
                out=h2sb[slh][:, 512:1024],
                in_=YMID[(tM * 128 - 1): (tM * 128 + 127), 512:1024].rearrange("(a p) c -> p a c", p=128)[:, 0, :]), s_h2[slh])
            V.wait(s_h2[slh], hv)
            xr = h2sb[slh]
            V.op(lambda e, xr=xr: e.bn_stats(lnb_stats[:, 0:6], xr[:, 0:512]))
            V.op(lambda e, xr=xr: e.bn_stats(lnb_stats[:, 6:12], xr[:, 512:1024]))
            v_stats = V.op(lambda e: e.bn_aggr(lnb_mv[:, 0:2], lnb_stats[:, 0:12]))
            SC.wait(sv, v_stats)
            v_sq = SC.inc(lambda e: e.activation(lnb_mv[:, 2:3], lnb_mv[:, 1:2], AF.Sqrt,
                                                 bias=eps_sb[:, 0:1]), ssc)
            V.wait(ssc, v_sq)
            V.op(lambda e: e.reciprocal(lnb_mv[:, 2:3], lnb_mv[:, 2:3]))
            V.op(lambda e: e.tensor_scalar(lnb_mv[:, 3:4], lnb_mv[:, 0:1], lnb_mv[:, 2:3], -1.0,
                                           op0=OP.mult, op1=OP.mult))
            h2_vals[slh] = V.op(lambda e, xr=xr: e.tensor_scalar(
                lnb_x[:], xr[:], lnb_mv[:, 2:3], lnb_mv[:, 3:4], op0=OP.mult, op1=OP.add))
            v_app = h2_vals[slh]
            for ct in range(8):
                pt, ptinfo = pt_pool[ct % 2]
                TE.wait(sv, v_app)
                TE.wait(sv, ptinfo["free"])
                v_tr = TE.inc(lambda e, pt=pt, ct=ct: e.transpose(
                    pt[:, 0:128], lnb_x[:, ct * 128:(ct + 1) * 128], idnb[:]), smm)
                V.wait(smm, v_tr)
                ptinfo["free"] = V.op(lambda e, pt=pt, ct=ct, ti=ti: e.tensor_scalar(
                    xt_sb[:, ct * 512 + ti * 128: ct * 512 + (ti + 1) * 128], pt[:, 0:128],
                    lgc_sb[:, ct:ct + 1], lbc_sb[:, ct:ct + 1], op0=OP.mult, op1=OP.add))
        xt_done = sv.n
        if qtr == 0:
            w1v = {0: issue_w1_load(0, 0)}
        for pb in range(22):
            slw = pb % 2
            wwid = 42 if pb == 21 else 128
            wv = w1v.pop(pb)
            nxt = pb + 1 if pb < 21 else (0 if qtr < 3 else None)
            if nxt is not None:
                w1v[nxt] = issue_w1_load(nxt, w1_last_use[1 - slw])
            pj, pinfo = pj_pool[0]
            TE.wait(sv, pinfo["free"])
            TE.wait(ssc, pinfo["free_sc"])
            TE.wait(sv, xt_done)
            TE.wait(s_w1[slw], wv)
            for ct in range(8):
                last = ct == 7
                fn = lambda e, pj=pj, ct=ct, slw=slw, wwid=wwid: e.matmul(
                    pj[0:wwid, 0:512],
                    w1a[slw][:, ct * 128: ct * 128 + wwid],
                    xt_sb[:, ct * 512:(ct + 1) * 512],
                    start=(ct == 0), stop=(ct == 7))
                if last:
                    v_a = TE.inc(fn, smm)
                else:
                    TE.op(fn)
            w1_last_use[slw] = v_a
            V.wait(smm, v_a)
            pinfo["free"] = V.op(lambda e, pj=pj, pb=pb, wwid=wwid: e.tensor_scalar(
                asb[0:wwid, :], pj[0:wwid, 0:512], b1a_sb[0:wwid, pb:pb + 1], None, op0=OP.add))
            v_asb = pinfo["free"]
            pg, pginfo = pj_pool[1]
            TE.wait(sv, pginfo["free"])
            TE.wait(ssc, pginfo["free_sc"])
            for ct in range(8):
                last = ct == 7
                fn = lambda e, pg=pg, ct=ct, slw=slw, wwid=wwid: e.matmul(
                    pg[0:wwid, 0:512],
                    w1g[slw][:, ct * 128: ct * 128 + wwid],
                    xt_sb[:, ct * 512:(ct + 1) * 512],
                    start=(ct == 0), stop=(ct == 7))
                if last:
                    v_g = TE.inc(fn, smm)
                else:
                    TE.op(fn)
            w1_last_use[slw] = v_g
            SC.wait(smm, v_g)
            SC.wait(sv, v_asb)
            SC.wait(smm, hT_mm_last)
            v_gel = SC.inc(lambda e, pg=pg, pb=pb, wwid=wwid: e.activation(
                lnb_x[0:wwid, 0:512], pg[0:wwid, 0:512], AF.Gelu,
                bias=b1g_sb[0:wwid, pb:pb + 1]), ssc)
            pginfo["free_sc"] = v_gel
            pginfo["free"] = 0
            V.wait(ssc, v_gel)
            V.op(lambda e, pb=pb, wwid=wwid: e.tensor_tensor(
                hT_sb[0:wwid, pb * 512:(pb + 1) * 512] if wwid == 42 else hT_sb[:, pb * 512:(pb + 1) * 512],
                asb[0:wwid, :] if wwid == 42 else asb[:],
                lnb_x[0:wwid, 0:512] if wwid == 42 else lnb_x[:, 0:512], op=OP.mult))
        hT_done = sv.n
        if debug and qtr == 0:
            DBGX = p.dout("DBGX", [128, 8 * 512], BF16)
            DBGH = p.dout("DBGH", [128, 21 * 512], BF16)
            DBGH2 = p.dout("DBGH2", [42, 512], BF16)
            s_dbg = p.sem("s_dbg")
            SY.wait(sv, hT_done)
            SY.dma(lambda e: e.dma_start(out=DBGX, in_=xt_sb), s_dbg)
            SY.dma(lambda e: e.dma_start(out=DBGH, in_=hT_sb[:, 0:21 * 512]), s_dbg)
            SY.dma(lambda e: e.dma_start(out=DBGH2, in_=hT_sb[0:42, 21 * 512:22 * 512]), s_dbg)
            DBGW = p.dout("DBGW", [128, 21 * 1024], BF16)
            SY.wait(s_w2, w2_val)
            SY.dma(lambda e: e.dma_start(out=DBGW, in_=w2_sb[:, 0:21 * 1024]), s_dbg)
            SY.wait(s_dbg, 64)
        for ti in range(4):
            tloc = qtr * 4 + ti
            ps, psinfo = ps_pool[ti % 2]
            TE.wait(sv, psinfo["free"])
            TE.wait(sv, hT_done)
            TE.wait(s_w2, w2_val)
            idx = 0
            for it in range(22):
                wwid = 42 if it == 21 else 128
                for half in range(2):
                    last = (it == 21 and half == 1)
                    fn = lambda e, ps=ps, it=it, half=half, ti=ti, wwid=wwid: e.matmul(
                        ps[:, half * 512:(half + 1) * 512],
                        hT_sb[0:wwid, it * 512 + ti * 128: it * 512 + (ti + 1) * 128],
                        w2_sb[0:wwid, it * 1024 + half * 512: it * 1024 + (half + 1) * 512],
                        start=(it == 0), stop=(it == 21))
                    if last:
                        v_w2 = TE.inc(fn, smm)
                    else:
                        TE.op(fn)
            hT_mm_last = v_w2
            slr = ti % 2
            tM = 4 + tloc
            SY.wait(sv, sv.n)
            yv = SY.dma(lambda e, tM=tM, slr=slr: e.dma_start(
                out=ysr_t[slr][:], in_=YMID_re[:, tM, :]), s_ysr[slr])
            if final_res:
                xv = SY.dma(lambda e, tloc=tloc, slr=slr: e.dma_start(
                    out=x1t[slr][:], in_=X1_re[:, tloc, :]), s_x1t[slr])
            V.wait(smm, v_w2)
            V.wait(s_ysr[slr], yv)
            V.wait(s_yo, yo_store)
            V.wait(s_ym, ymid_all)
            V.op(lambda e, ps=ps, slr=slr: e.scalar_tensor_tensor(
                yo_t[:], ps[:], 1.0, ysr_t[slr][:], op0=OP.bypass, op1=OP.add))
            psinfo["free"] = sv.n
            V.wait(s_b2r, b2r_val)
            V.op(lambda e: e.tensor_tensor(yo_t[:], yo_t[:], b2r_sb[:, :], op=OP.add))
            if final_res:
                V.wait(s_x1t[slr], xv)
                V.op(lambda e, slr=slr: e.tensor_tensor(yo_t[:], yo_t[:], x1t[slr][:], op=OP.add))
            v_fin = sv.n
            SY.wait(sv, v_fin)
            yo_store = SY.dma(lambda e, tloc=tloc: e.dma_start(
                out=YOUT_re[:, tloc, :], in_=yo_t[:]), s_yo)

    SY.wait(s_yo, s_yo.n)
    return p.build()


def consts_B(core):
    b, r = core // 4, core % 4
    start = r * TC
    pos = (start + np.arange(-1024, TC)).astype(np.float32)
    half_idx = np.arange(0, 64, 2, dtype=np.float32)
    inv_freq = 1.0 / 10000.0 ** (half_idx / 64)
    freqs = np.concatenate([pos[:, None] * inv_freq[None, :]] * 2, axis=-1)
    base = (half_idx + 0.4 * 64) / (1.4 * 64)
    power = (pos - T // 2) / 256.0
    scale = np.concatenate([base[None, :] ** power[:, None]] * 2, axis=-1).astype(np.float32)
    cosf = np.cos(freqs).astype(np.float32)
    sinf = np.sin(freqs).astype(np.float32)
    csq = (cosf * scale * 8.0)
    ssq = (sinf * scale * 8.0)
    csk = (cosf / scale)
    ssk = (sinf / scale)

    def lay_rot(tab):
        t8 = np.tile(tab, (1, 8))
        return t8.reshape(24, 128, 512).transpose(1, 0, 2).reshape(128, 24 * 512).astype(BF).copy()

    masks = np.zeros((6, 512, 1024), np.float32)
    for beta in range(6):
        qrow = start - 1024 + 512 * beta + np.arange(512)
        krow = start - 1536 + 512 * beta + np.arange(1024)
        d = qrow[:, None] - krow[None, :]
        keep = (d >= 0) & (d <= 512) & (krow[None, :] >= 0)
        bad = ~keep.any(axis=1)
        keep[bad, 512 + np.arange(512)[bad]] = True
        masks[beta] = np.where(keep, 0.0, -1e30)
    return {
        "CSQ": lay_rot(csq), "SSQ": lay_rot(ssq), "CSK": lay_rot(csk), "SSK": lay_rot(ssk),
        "MASKS": masks.astype(BF), "IDN": np.eye(128, dtype=BF),
    }


def params_B(z, layer):
    z = {k: np.asarray(v) for k, v in z.items() if k.startswith("L0_ffn")}
    b1 = z["L0_ffn_b1"][layer]
    b1a = np.zeros((128, 22), np.float32)
    b1g = np.zeros((128, 22), np.float32)
    for pb in range(22):
        w = 42 if pb == 21 else 128
        b1a[:w, pb] = b1[pb * 128: pb * 128 + w]
        b1g[:w, pb] = b1[2730 + pb * 128: 2730 + pb * 128 + w]
    lg = z["L0_ffn_ln_g"][layer].reshape(8, 128).T.copy()
    lb = z["L0_ffn_ln_b"][layer].reshape(8, 128).T.copy()
    w2 = z["L0_ffn_w2"][layer]
    return {
        "WQKV": z["L0_ffn_wqkv"][layer],
        "WO": z["L0_ffn_wo"][layer],
        "W1": z["L0_ffn_w1"][layer],
        "W2F": np.ascontiguousarray(w2[0:2688]),
        "W2L": np.ascontiguousarray(w2[2688:2730]),
        "B1A": b1a, "B1G": b1g,
        "B2R": np.ascontiguousarray(np.broadcast_to(z["L0_ffn_b2"][layer], (128, C))),
        "LGC": lg, "LBC": lb,
        "QSR": np.ascontiguousarray(np.broadcast_to(np.tile(z["L0_ffn_q_scale"][layer], 8), (128, 512))),
        "KSR": np.ascontiguousarray(np.broadcast_to(np.tile(z["L0_ffn_k_scale"][layer], 8), (128, 512))),
    }


def halo_rows(full_b, core, halo):
    b, r = core // 4, core % 4
    start = r * TC
    lo = start - halo
    out = np.zeros((halo + TC, full_b.shape[1]), np.float32)
    src_lo = max(0, lo)
    out[src_lo - lo:] = full_b[src_lo: start + TC]
    return out


HALO_C = 128
NTC = 17


def build_C():
    p = Prog()
    TE, V, SC, SY, G = p.e["tensor"], p.e["vector"], p.e["scalar"], p.e["sync"], p.e["gpsimd"]

    X2H = p.din("X2H", [HALO_C + TC, C], F32)
    WQ1 = p.din("WQ1", [C, C], F32)
    WK1 = p.din("WK1", [C, C], F32)
    WV1 = p.din("WV1", [C, C], F32)
    WO1 = p.din("WO1", [C, C], F32)
    BQC = p.din("BQC", [128, 8], F32)
    BKC = p.din("BKC", [128, 8], F32)
    BVR = p.din("BVR", [128, C], F32)
    BOR = p.din("BOR", [128, C], F32)
    MASKC = p.din("MASKC", [16, 128, 256], BF16)
    IDN = p.din("IDN", [128, 128], BF16)
    X3 = p.dout("X3", [TC, C], F32)
    X3_re = X3.rearrange("(a p) c -> p a c", p=128)
    XH_re = X2H.rearrange("(a p) c -> p a c", p=128)

    XW = NTC * 128
    xT = p.sbuf("xT", [128, 8 * XW], BF16)
    wq_sb = p.sbuf("wq_sb", [128, 8 * 1024], BF16)
    wk_sb = p.sbuf("wk_sb", [128, 8 * 1024], BF16)
    wv_sb = p.sbuf("wv_sb", [128, 8 * 1024], BF16)
    wo_sb = p.sbuf("wo_sb", [128, 8 * 1024], BF16)
    bqc = p.sbuf("bqc", [128, 8], F32)
    bkc = p.sbuf("bkc", [128, 8], F32)
    bvr = p.sbuf("bvr", [128, C], F32)
    bor = p.sbuf("bor", [128, C], F32)
    idnb = p.sbuf("idnb", [128, 128], BF16)
    qT = p.sbuf("qT", [128, 8 * 128], BF16)
    kT = [p.sbuf("kTc0", [128, 8 * 128], BF16), p.sbuf("kTc1", [128, 8 * 128], BF16)]
    vb = [p.sbuf("vbc0", [128, C], BF16), p.sbuf("vbc1", [128, C], BF16)]
    msk = [p.sbuf("mskc0", [128, 256], BF16), p.sbuf("mskc1", [128, 256], BF16)]
    simsb = p.sbuf("simc", [128, 256], F32)
    nmx = p.sbuf("nmxc", [128, 1], F32)
    zsum = p.sbuf("zsumc", [128, 1], F32)
    esb = p.sbuf("esbc", [128, 256], BF16)
    eT = p.sbuf("eTc", [128, 256], BF16)
    osb = p.sbuf("osbc", [128, 8 * 128], BF16)
    xt_f32 = [p.sbuf("xtf0", [128, C], F32), p.sbuf("xtf1", [128, C], F32)]
    x2t = [p.sbuf("x2t0", [128, C], F32), p.sbuf("x2t1", [128, C], F32)]
    xo_t = p.sbuf("xo_t", [128, C], F32)
    lnc_stats = p.sbuf("lnc_stats", [128, 12], F32)
    lnc_mv = p.sbuf("lnc_mv", [128, 4], F32)
    lnc_x = p.sbuf("lnc_x", [128, C], BF16)
    eps_sb = p.sbuf("eps_c", [128, 1], F32)

    pj_pool = [(p.psum("pjC0", [128, 512], F32), {"free": 0}),
               (p.psum("pjC1", [128, 512], F32), {"free": 0})]
    ps_pool = [(p.psum("psC0", [128, 1024], F32), {"free": 0}),
               (p.psum("psC1", [128, 1024], F32), {"free": 0})]
    pt_pool = [(p.psum("ptC0", [128, 1024], BF16), {"free": 0}),
               (p.psum("ptC1", [128, 1024], BF16), {"free": 0})]

    s_const = p.sem("s_const")
    s_constG = p.sem("s_constG")
    s_x = [p.sem("s_x0"), p.sem("s_x1c")]
    s_msk = [p.sem("s_mskc0"), p.sem("s_mskc1")]
    s_x2 = [p.sem("s_x2t0"), p.sem("s_x2t1")]
    s_x3 = p.sem("s_x3")
    smm = p.sem("smm")
    sv = p.sem("sv")
    ssc = p.sem("ssc")
    sgp = p.sem("sgp")
    V.selfsem = sv
    SC.selfsem = ssc
    G.selfsem = sgp

    eps_val = G.inc(lambda e: e.memset(eps_sb[:], float(EPS)), sgp)
    for dst, srcap in ((wq_sb, WQ1), (wk_sb, WK1), (wv_sb, WV1), (wo_sb, WO1)):
        Wre = srcap.rearrange("(a p) o -> p a o", p=128)
        G.dma(lambda e, dst=dst, Wre=Wre: e.dma_start(
            out=dst.ap().rearrange("p (a o) -> p a o", o=1024), in_=Wre), s_constG)
    wconst_val = s_constG.n
    for dst, srcap in ((bqc, BQC), (bkc, BKC), (bvr, BVR), (bor, BOR), (idnb, IDN)):
        SY.dma(lambda e, dst=dst, srcap=srcap: e.dma_start(out=dst[:], in_=srcap), s_const)
    consts_val = s_const.n

    x_vals = [0] * NTC
    for tt in range(2):
        x_vals[tt] = SY.dma(lambda e, tt=tt: e.dma_start(
            out=xt_f32[tt % 2][:], in_=XH_re[:, tt, :]), s_x[tt % 2])
    for tt in range(NTC):
        sl = tt % 2
        xr = xt_f32[sl]
        V.wait(s_x[sl], x_vals[tt])
        V.op(lambda e, xr=xr: e.bn_stats(lnc_stats[:, 0:6], xr[:, 0:512]))
        V.op(lambda e, xr=xr: e.bn_stats(lnc_stats[:, 6:12], xr[:, 512:1024]))
        v_stats = V.op(lambda e: e.bn_aggr(lnc_mv[:, 0:2], lnc_stats[:, 0:12]))
        SC.wait(sv, v_stats)
        SC.wait(sgp, eps_val)
        v_sq = SC.inc(lambda e: e.activation(lnc_mv[:, 2:3], lnc_mv[:, 1:2], AF.Sqrt,
                                             bias=eps_sb[:, 0:1]), ssc)
        V.wait(ssc, v_sq)
        V.op(lambda e: e.reciprocal(lnc_mv[:, 2:3], lnc_mv[:, 2:3]))
        V.op(lambda e: e.tensor_scalar(lnc_mv[:, 3:4], lnc_mv[:, 0:1], lnc_mv[:, 2:3], -1.0,
                                       op0=OP.mult, op1=OP.mult))
        v_app = V.op(lambda e, xr=xr: e.tensor_scalar(lnc_x[:], xr[:], lnc_mv[:, 2:3],
                                                      lnc_mv[:, 3:4], op0=OP.mult, op1=OP.add))
        if tt + 2 < NTC:
            SY.wait(sv, v_app)
            x_vals[tt + 2] = SY.dma(lambda e, tt=tt, sl=sl: e.dma_start(
                out=xt_f32[sl][:], in_=XH_re[:, tt + 2, :]), s_x[sl])
        for ct in range(8):
            pt, ptinfo = pt_pool[ct % 2]
            TE.wait(s_const, consts_val)
            TE.wait(sv, v_app)
            TE.wait(sv, ptinfo["free"])
            v_tr = TE.inc(lambda e, pt=pt, ct=ct: e.transpose(
                pt[:, 0:128], lnc_x[:, ct * 128:(ct + 1) * 128], idnb[:]), smm)
            V.wait(smm, v_tr)
            ptinfo["free"] = V.op(lambda e, pt=pt, ct=ct, tt=tt: e.tensor_copy(
                xT[:, ct * XW + tt * 128: ct * XW + (tt + 1) * 128], pt[:, 0:128]))
    xT_done = sv.n

    kv_last_use = [0, 0]
    q_last_use = 0
    eT_last_use = 0
    esb_last_use = 0
    osb_last_use = 0
    x3_store = 0
    msk_free = [0, 0]

    def project_kv(tile, sl):
        for ctp in range(8):
            pj, pinfo = pj_pool[ctp % 2]
            TE.wait(sv, pinfo["free"])
            TE.wait(sv, xT_done)
            TE.wait(s_constG, wconst_val)
            for ct in range(8):
                last = ct == 7
                fn = lambda e, pj=pj, ct=ct, ctp=ctp, tile=tile: e.matmul(
                    pj[:, 0:128],
                    wk_sb[:, ct * 1024 + ctp * 128: ct * 1024 + (ctp + 1) * 128],
                    xT[:, ct * XW + tile * 128: ct * XW + (tile + 1) * 128],
                    start=(ct == 0), stop=(ct == 7))
                if last:
                    v_mm = TE.inc(fn, smm)
                else:
                    TE.op(fn)
            V.wait(smm, v_mm)
            V.wait(smm, kv_last_use[sl])
            pinfo["free"] = V.op(lambda e, pj=pj, ctp=ctp, sl=sl: e.tensor_scalar(
                kT[sl][:, ctp * 128:(ctp + 1) * 128], pj[:, 0:128],
                bkc[:, ctp:ctp + 1], None, op0=OP.add))
        for half in range(2):
            pv, pvinfo = ps_pool[half % 2]
            TE.wait(sv, pvinfo["free"])
            for ct in range(8):
                last = ct == 7
                fn = lambda e, pv=pv, ct=ct, half=half, tile=tile: e.matmul(
                    pv[:, 0:512],
                    xT[:, ct * XW + tile * 128: ct * XW + (tile + 1) * 128],
                    wv_sb[:, ct * 1024 + half * 512: ct * 1024 + (half + 1) * 512],
                    start=(ct == 0), stop=(ct == 7))
                if last:
                    v_mm = TE.inc(fn, smm)
                else:
                    TE.op(fn)
            V.wait(smm, v_mm)
            V.wait(smm, kv_last_use[sl])
            pvinfo["free"] = V.op(lambda e, pv=pv, half=half, sl=sl: e.scalar_tensor_tensor(
                vb[sl][:, half * 512:(half + 1) * 512], pv[:, 0:512], 1.0,
                bvr[:, half * 512:(half + 1) * 512], op0=OP.bypass, op1=OP.add))

    project_kv(0, 0)

    for qb in range(16):
        sl = (qb + 1) % 2
        slp = qb % 2
        project_kv(qb + 1, sl)
        for ctp in range(8):
            pj, pinfo = pj_pool[ctp % 2]
            TE.wait(sv, pinfo["free"])
            TE.wait(smm, q_last_use)
            for ct in range(8):
                last = ct == 7
                fn = lambda e, pj=pj, ct=ct, ctp=ctp, qb=qb: e.matmul(
                    pj[:, 0:128],
                    wq_sb[:, ct * 1024 + ctp * 128: ct * 1024 + (ctp + 1) * 128],
                    xT[:, ct * XW + (qb + 1) * 128: ct * XW + (qb + 2) * 128],
                    start=(ct == 0), stop=(ct == 7))
                if last:
                    v_mm = TE.inc(fn, smm)
                else:
                    TE.op(fn)
            V.wait(smm, v_mm)
            pinfo["free"] = V.op(lambda e, pj=pj, ctp=ctp: e.tensor_scalar(
                qT[:, ctp * 128:(ctp + 1) * 128], pj[:, 0:128],
                bqc[:, ctp:ctp + 1], 0.125, op0=OP.add, op1=OP.mult))
        q_done = sv.n
        slm = qb % 2
        SY.wait(sv, msk_free[slm])
        mv = SY.dma(lambda e, qb=qb, slm=slm: e.dma_start(
            out=msk[slm][:], in_=MASKC[qb, :, :]), s_msk[slm])
        for g in range(8):
            po, poinfo = pj_pool[g % 2]
            for hh in range(2):
                h = 2 * g + hh
                hct, hof = h // 2, (h % 2) * 64
                ps, psinfo = ps_pool[h % 2]
                TE.wait(sv, psinfo["free"])
                TE.wait(sv, q_done)
                TE.op(lambda e, ps=ps, hct=hct, hof=hof, slp=slp: e.matmul(
                    ps[:, 0:128],
                    qT[hof:hof + 64, hct * 128:(hct + 1) * 128],
                    kT[slp][hof:hof + 64, hct * 128:(hct + 1) * 128],
                    start=True, stop=True))
                v_mm = TE.inc(lambda e, ps=ps, hct=hct, hof=hof, sl=sl: e.matmul(
                    ps[:, 128:256],
                    qT[hof:hof + 64, hct * 128:(hct + 1) * 128],
                    kT[sl][hof:hof + 64, hct * 128:(hct + 1) * 128],
                    start=True, stop=True), smm)
                V.wait(smm, v_mm)
                V.wait(ssc, ssc.n)
                V.wait(s_msk[slm], mv)
                v_msku = V.op(lambda e, ps=ps, slm=slm: e.scalar_tensor_tensor(
                    simsb[:], ps[:, 0:256], 1.0, msk[slm][:], op0=OP.bypass, op1=OP.add))
                msk_free[slm] = v_msku
                v_nm = V.op(lambda e: e.reduce_max(nmx[:], simsb[:], axis=AX.X, negate=True))
                psinfo["free"] = v_nm
                SC.wait(sv, v_nm)
                SC.wait(smm, esb_last_use)
                v_exp = SC.inc(lambda e: e.activation(esb[:], simsb[:], AF.Exp,
                                                      bias=nmx[:, 0:1], accum_out=zsum[:, 0:1]), ssc)
                V.wait(ssc, v_exp)
                V.op(lambda e: e.reciprocal(zsum[:], zsum[:]))
                v_en = V.op(lambda e: e.tensor_scalar(esb[:], esb[:], zsum[:, 0:1], None, op0=OP.mult))
                pt, ptinfo = pt_pool[h % 2]
                TE.wait(sv, v_en)
                TE.wait(sv, ptinfo["free"])
                TE.wait(smm, eT_last_use)
                TE.op(lambda e, pt=pt: e.transpose(pt[:, 0:128], esb[:, 0:128], idnb[:]))
                v_tr2 = TE.inc(lambda e, pt=pt: e.transpose(pt[:, 128:256], esb[:, 128:256], idnb[:]), smm)
                esb_last_use = v_tr2
                V.wait(smm, v_tr2)
                ptinfo["free"] = V.op(lambda e, pt=pt: e.tensor_copy(eT[:], pt[:, 0:256]))
                v_eT = sv.n
                TE.wait(sv, v_eT)
                if hh == 0:
                    TE.wait(sv, poinfo["free"])
                TE.op(lambda e, po=po, h=h, hof=hof, slp=slp: e.matmul(
                    po[hof:hof + 64, 0:128], vb[slp][:, h * 64:(h + 1) * 64],
                    eT[:, 0:128], start=True, stop=False))
                v_mm2 = TE.inc(lambda e, po=po, h=h, hof=hof, sl=sl: e.matmul(
                    po[hof:hof + 64, 0:128], vb[sl][:, h * 64:(h + 1) * 64],
                    eT[:, 128:256], start=False, stop=True), smm)
                eT_last_use = v_mm2
            V.wait(smm, v_mm2)
            V.wait(smm, osb_last_use)
            poinfo["free"] = V.op(lambda e, po=po, g=g: e.tensor_copy(
                osb[:, g * 128:(g + 1) * 128], po[:, 0:128]))
        v_osb = sv.n
        kv_last_use[slp] = v_mm2
        ps, psinfo = ps_pool[qb % 2]
        TE.wait(sv, psinfo["free"])
        TE.wait(sv, v_osb)
        for g8 in range(8):
            for half in range(2):
                last = (g8 == 7 and half == 1)
                fn = lambda e, ps=ps, g8=g8, half=half: e.matmul(
                    ps[:, half * 512:(half + 1) * 512],
                    osb[:, g8 * 128:(g8 + 1) * 128],
                    wo_sb[:, g8 * 1024 + half * 512: g8 * 1024 + (half + 1) * 512],
                    start=(g8 == 0), stop=(g8 == 7))
                if last:
                    v_wo = TE.inc(fn, smm)
                else:
                    TE.op(fn)
        osb_last_use = v_wo
        slr = qb % 2
        SY.wait(sv, sv.n)
        xv = SY.dma(lambda e, qb=qb, slr=slr: e.dma_start(
            out=x2t[slr][:], in_=XH_re[:, qb + 1, :]), s_x2[slr])
        V.wait(smm, v_wo)
        V.wait(s_x2[slr], xv)
        V.wait(s_x3, x3_store)
        V.op(lambda e, ps=ps, slr=slr: e.scalar_tensor_tensor(
            xo_t[:], ps[:], 1.0, x2t[slr][:], op0=OP.bypass, op1=OP.add))
        psinfo["free"] = sv.n
        v_fin = V.op(lambda e: e.tensor_tensor(xo_t[:], xo_t[:], bor[:], op=OP.add))
        SY.wait(sv, v_fin)
        x3_store = SY.dma(lambda e, qb=qb: e.dma_start(out=X3_re[:, qb, :], in_=xo_t[:]), s_x3)

    SY.wait(s_x3, s_x3.n)
    return p.build()


def consts_C(core):
    b, r = core // 4, core % 4
    start = r * TC
    masks = np.zeros((16, 128, 256), np.float32)
    for qb in range(16):
        qrow = start + qb * 128 + np.arange(128)
        krow = start + (qb - 1) * 128 + np.arange(256)
        d = qrow[:, None] - krow[None, :]
        keep = (d >= 0) & (d <= 128) & (krow[None, :] >= 0)
        masks[qb] = np.where(keep, 0.0, -1e30)
    return {"MASKC": masks.astype(BF), "IDN": np.eye(128, dtype=BF)}


def params_C(z):
    z = {k: np.asarray(v) for k, v in z.items() if k.startswith("L1_attn")}
    return {
        "WQ1": z["L1_attn_Wq"], "WK1": z["L1_attn_Wk"], "WV1": z["L1_attn_Wv"],
        "WO1": z["L1_attn_Wo"],
        "BQC": z["L1_attn_bq"].reshape(8, 128).T.copy(),
        "BKC": z["L1_attn_bk"].reshape(8, 128).T.copy(),
        "BVR": np.ascontiguousarray(np.broadcast_to(z["L1_attn_bv"], (128, C))),
        "BOR": np.ascontiguousarray(np.broadcast_to(z["L1_attn_bo"], (128, C))),
    }


def build_D():
    p = Prog()
    TE, V, SC, SY, G = p.e["tensor"], p.e["vector"], p.e["scalar"], p.e["sync"], p.e["gpsimd"]

    X3R = p.din("X3R", [TC, C], F32)
    TW1 = p.din("TW1", [4096, C], F32)
    TW2 = p.din("TW2", [4096, C], F32)
    TH1 = p.din("TH1", [4096, 4096], F32)
    TH2 = p.din("TH2", [4096, 4096], F32)
    TW3 = p.din("TW3", [C, 4096], F32)
    S1C = p.din("S1C", [128, 32], F32)
    S2C = p.din("S2C", [128, 32], F32)
    SH1C = p.din("SH1C", [128, 32], F32)
    SH2C = p.din("SH2C", [128, 32], F32)
    S3R = p.din("S3R", [128, C], F32)
    IDN = p.din("IDN", [128, 128], BF16)
    X4 = p.dout("X4", [TC, C], F32)
    X4_re = X4.rearrange("(a p) c -> p a c", p=128)
    X3_re = X3R.rearrange("(a p) c -> p a c", p=128)

    HB = 1024

    xtD = p.sbuf("xtD", [128, 8 * HB], BF16)
    hD = p.sbuf("hD", [128, 32 * HB], BF16)
    gD = p.sbuf("gD", [128, 32 * HB], BF16)
    w3T = gD
    wt_n = [p.sbuf("wtn0", [128, 32 * 128], BF16), p.sbuf("wtn1", [128, 32 * 128], BF16)]
    wt_t = [p.sbuf("wtt0", [128, 32 * 128], BF16), p.sbuf("wtt1", [128, 32 * 128], BF16)]
    t1s_sb = p.sbuf("t1sD", [128, HB], BF16)
    s1c = p.sbuf("s1c", [128, 32], F32)
    s2c = p.sbuf("s2c", [128, 32], F32)
    sh1c = p.sbuf("sh1c", [128, 32], F32)
    sh2c = p.sbuf("sh2c", [128, 32], F32)
    s3r = p.sbuf("s3r", [128, C], F32)
    idnb = p.sbuf("idnd", [128, 128], BF16)
    xt_f32 = [p.sbuf("xdf0", [128, C], F32), p.sbuf("xdf1", [128, C], F32)]
    x3t = [p.sbuf("x3t0", [128, C], F32), p.sbuf("x3t1", [128, C], F32)]
    xo_t = p.sbuf("xoD", [128, C], F32)
    lnd_stats = p.sbuf("lnd_stats", [128, 12], F32)
    lnd_mv = p.sbuf("lnd_mv", [128, 4], F32)
    lnd_x = p.sbuf("lnd_x", [128, C], BF16)
    eps_sb = p.sbuf("eps_d", [128, 1], F32)

    pj_pool = [(p.psum("pjD0", [128, 1024], F32), {"free": 0, "free_sc": 0}),
               (p.psum("pjD1", [128, 1024], F32), {"free": 0, "free_sc": 0})]
    pt_pool = [(p.psum("ptD0", [128, 1024], BF16), {"free": 0}),
               (p.psum("ptD1", [128, 1024], BF16), {"free": 0})]

    s_const = p.sem("s_const")
    s_x = [p.sem("s_xd0"), p.sem("s_xd1")]
    s_w = [p.sem("s_wd0"), p.sem("s_wd1")]
    s_x3 = [p.sem("s_x3d0"), p.sem("s_x3d1")]
    s_x4 = p.sem("s_x4")
    smm = p.sem("smm")
    sv = p.sem("sv")
    ssc = p.sem("ssc")
    sgp = p.sem("sgp")
    V.selfsem = sv
    SC.selfsem = ssc
    G.selfsem = sgp

    eps_val = G.inc(lambda e: e.memset(eps_sb[:], float(EPS)), sgp)
    for dst, srcap in ((s1c, S1C), (s2c, S2C), (sh1c, SH1C), (sh2c, SH2C),
                       (s3r, S3R), (idnb, IDN)):
        SY.dma(lambda e, dst=dst, srcap=srcap: e.dma_start(out=dst[:], in_=srcap), s_const)
    consts_val = s_const.n

    TW1re = TW1.rearrange("(a p) i -> p a i", p=128)
    TW2re = TW2.rearrange("(a p) i -> p a i", p=128)
    TH1re = TH1.rearrange("(a p) i -> p a i", p=128)
    TH2re = TH2.rearrange("(a p) i -> p a i", p=128)
    TW3re = TW3.rearrange("(a p) i -> p a i", p=128)

    w_last_use = [0, 0]

    def load_w_rows(Wre, ot, nin, slw, guard_mm):
        G.wait(smm, guard_mm)
        return G.dma(lambda e, Wre=Wre, ot=ot, nin=nin, slw=slw: e.dma_start(
            out=wt_n[slw][:, 0:nin], in_=Wre[:, ot, 0:nin]), s_w[slw])

    def transpose_w(nin, slw, wv, pt_guard_extra=0):
        vals = []
        for it in range(nin // 128):
            pt, ptinfo = pt_pool[it % 2]
            TE.wait(s_w[slw], wv)
            TE.wait(sv, ptinfo["free"])
            v_tr = TE.inc(lambda e, pt=pt, it=it, slw=slw: e.transpose(
                pt[:, 0:128], wt_n[slw][:, it * 128:(it + 1) * 128], idnb[:]), smm)
            V.wait(smm, v_tr)
            ptinfo["free"] = V.op(lambda e, pt=pt, it=it, slw=slw: e.tensor_copy(
                wt_t[slw][:, it * 128:(it + 1) * 128], pt[:, 0:128]))
        return sv.n

    x4_store = 0
    x3_vals = [0, 0]
    app_vals = [0, 0]
    hD_mm_last = 0
    gD_mm_last = 0
    xt_mm_last = 0

    for half in range(2):
        for ti in range(8):
            tloc = half * 8 + ti
            sl = ti % 2
            SY.wait(sv, app_vals[sl])
            xv = SY.dma(lambda e, tloc=tloc, sl=sl: e.dma_start(
                out=xt_f32[sl][:], in_=X3_re[:, tloc, :]), s_x[sl])
            x3_vals[sl] = xv
            xr = xt_f32[sl]
            V.wait(s_x[sl], xv)
            V.op(lambda e, xr=xr: e.bn_stats(lnd_stats[:, 0:6], xr[:, 0:512]))
            V.op(lambda e, xr=xr: e.bn_stats(lnd_stats[:, 6:12], xr[:, 512:1024]))
            v_stats = V.op(lambda e: e.bn_aggr(lnd_mv[:, 0:2], lnd_stats[:, 0:12]))
            SC.wait(sv, v_stats)
            SC.wait(sgp, eps_val)
            v_sq = SC.inc(lambda e: e.activation(lnd_mv[:, 2:3], lnd_mv[:, 1:2], AF.Sqrt,
                                                 bias=eps_sb[:, 0:1]), ssc)
            V.wait(ssc, v_sq)
            V.op(lambda e: e.reciprocal(lnd_mv[:, 2:3], lnd_mv[:, 2:3]))
            V.op(lambda e: e.tensor_scalar(lnd_mv[:, 3:4], lnd_mv[:, 0:1], lnd_mv[:, 2:3], -1.0,
                                           op0=OP.mult, op1=OP.mult))
            v_app = V.op(lambda e, xr=xr: e.tensor_scalar(lnd_x[:], xr[:], lnd_mv[:, 2:3],
                                                          lnd_mv[:, 3:4], op0=OP.mult, op1=OP.add))
            app_vals[sl] = v_app
            for ct in range(8):
                pt, ptinfo = pt_pool[ct % 2]
                TE.wait(s_const, consts_val)
                TE.wait(sv, v_app)
                TE.wait(sv, ptinfo["free"])
                TE.wait(smm, xt_mm_last)
                v_tr = TE.inc(lambda e, pt=pt, ct=ct: e.transpose(
                    pt[:, 0:128], lnd_x[:, ct * 128:(ct + 1) * 128], idnb[:]), smm)
                V.wait(smm, v_tr)
                ptinfo["free"] = V.op(lambda e, pt=pt, ct=ct, ti=ti: e.tensor_copy(
                    xtD[:, ct * HB + ti * 128: ct * HB + (ti + 1) * 128], pt[:, 0:128]))
        xt_done = sv.n

        wv = load_w_rows(TW1re, 0, 1024, 0, w_last_use[0])
        for ot in range(32):
            slw = ot % 2
            tr_done = transpose_w(1024, slw, wv)
            if ot < 31:
                nslw = 1 - slw
                wv = load_w_rows(TW1re, ot + 1, 1024, nslw, w_last_use[nslw])
            pj, pinfo = pj_pool[0]
            TE.wait(sv, pinfo["free"])
            TE.wait(ssc, pinfo["free_sc"])
            TE.wait(sv, max(tr_done, xt_done))
            for ct in range(8):
                for tch in range(2):
                    last = (ct == 7 and tch == 1)
                    fn = lambda e, pj=pj, ct=ct, tch=tch, slw=slw: e.matmul(
                        pj[:, tch * 512:(tch + 1) * 512],
                        wt_t[slw][:, ct * 128:(ct + 1) * 128],
                        xtD[:, ct * HB + tch * 512: ct * HB + (tch + 1) * 512],
                        start=(ct == 0), stop=(ct == 7))
                    if last:
                        v_t1 = TE.inc(fn, smm)
                    else:
                        TE.op(fn)
            w_last_use[slw] = v_t1
            SC.wait(smm, v_t1)
            SC.wait(sv, sv.n)
            v_sil = SC.inc(lambda e, pj=pj, ot=ot: e.activation(
                t1s_sb[:], pj[:], AF.Silu, scale=s1c[:, ot:ot + 1]), ssc)
            pinfo["free_sc"] = v_sil
            pinfo["free"] = 0
            slw2 = ot % 2
            wv2 = load_w_rows(TW2re, ot, 1024, slw2, w_last_use[slw2])
            tr2_done = transpose_w(1024, slw2, wv2)
            pg, pginfo = pj_pool[1]
            TE.wait(sv, pginfo["free"])
            TE.wait(ssc, pginfo["free_sc"])
            TE.wait(sv, tr2_done)
            for ct in range(8):
                for tch in range(2):
                    last = (ct == 7 and tch == 1)
                    fn = lambda e, pg=pg, ct=ct, tch=tch, slw2=slw2: e.matmul(
                        pg[:, tch * 512:(tch + 1) * 512],
                        wt_t[slw2][:, ct * 128:(ct + 1) * 128],
                        xtD[:, ct * HB + tch * 512: ct * HB + (tch + 1) * 512],
                        start=(ct == 0), stop=(ct == 7))
                    if last:
                        v_t2 = TE.inc(fn, smm)
                    else:
                        TE.op(fn)
            w_last_use[slw2] = v_t2
            V.wait(smm, v_t2)
            V.wait(ssc, v_sil)
            V.wait(smm, hD_mm_last)
            pginfo["free"] = V.op(lambda e, pg=pg, ot=ot: e.scalar_tensor_tensor(
                hD[:, ot * HB:(ot + 1) * HB], pg[:], s2c[:, ot:ot + 1], t1s_sb[:],
                op0=OP.mult, op1=OP.mult))
            pginfo["free_sc"] = 0
        h_done = sv.n

        for stage, (Wre, scol, dst, guard_name) in enumerate((
                (TH1re, sh1c, gD, "g"), (TH2re, sh2c, hD, "u"))):
            src_h = hD if stage == 0 else gD
            dst_guard = gD_mm_last if stage == 0 else smm.n
            wv = load_w_rows(Wre, 0, 4096, 0, w_last_use[0])
            for ot in range(32):
                slw = ot % 2
                tr_done = transpose_w(4096, slw, wv)
                if ot < 31:
                    nslw = 1 - slw
                    wv = load_w_rows(Wre, ot + 1, 4096, nslw, w_last_use[nslw])
                pj, pinfo = pj_pool[ot % 2]
                TE.wait(sv, pinfo["free"])
                TE.wait(ssc, pinfo["free_sc"])
                TE.wait(sv, tr_done)
                if stage == 0:
                    TE.wait(sv, h_done)
                else:
                    TE.wait(ssc, g_done)
                for ct in range(32):
                    for tch in range(2):
                        last = (ct == 31 and tch == 1)
                        fn = lambda e, pj=pj, ct=ct, tch=tch, slw=slw, src_h=src_h: e.matmul(
                            pj[:, tch * 512:(tch + 1) * 512],
                            wt_t[slw][:, ct * 128:(ct + 1) * 128],
                            src_h[:, ct * HB + tch * 512: ct * HB + (tch + 1) * 512],
                            start=(ct == 0), stop=(ct == 31))
                        if last:
                            v_mm = TE.inc(fn, smm)
                        else:
                            TE.op(fn)
                w_last_use[slw] = v_mm
                SC.wait(smm, v_mm)
                SC.wait(smm, dst_guard)
                v_gel = SC.inc(lambda e, pj=pj, ot=ot, dst=dst, scol=scol: e.activation(
                    dst[:, ot * HB:(ot + 1) * HB], pj[:], AF.Gelu,
                    scale=scol[:, ot:ot + 1]), ssc)
                pinfo["free_sc"] = v_gel
                pinfo["free"] = 0
            if stage == 0:
                g_done = ssc.n
            else:
                u_done = ssc.n

        w3_done = 0
        for ot8 in range(8):
            slw = ot8 % 2
            wv = load_w_rows(TW3re, ot8, 4096, slw, w_last_use[slw])
            for it in range(32):
                pt, ptinfo = pt_pool[it % 2]
                TE.wait(s_w[slw], wv)
                TE.wait(sv, ptinfo["free"])
                TE.wait(ssc, u_done)
                v_tr = TE.inc(lambda e, pt=pt, it=it, slw=slw: e.transpose(
                    pt[:, 0:128], wt_n[slw][:, it * 128:(it + 1) * 128], idnb[:]), smm)
                w_last_use[slw] = v_tr
                V.wait(smm, v_tr)
                ptinfo["free"] = V.op(lambda e, pt=pt, it=it, ot8=ot8: e.tensor_copy(
                    w3T[:, it * HB + ot8 * 128: it * HB + ot8 * 128 + 128], pt[:, 0:128]))
        w3_done = sv.n

        for ti in range(8):
            tloc = half * 8 + ti
            pj, pinfo = pj_pool[ti % 2]
            TE.wait(sv, pinfo["free"])
            TE.wait(ssc, pinfo["free_sc"])
            TE.wait(sv, w3_done)
            TE.wait(ssc, u_done)
            for it in range(32):
                for tch in range(2):
                    last = (it == 31 and tch == 1)
                    fn = lambda e, pj=pj, it=it, tch=tch, ti=ti: e.matmul(
                        pj[:, tch * 512:(tch + 1) * 512],
                        hD[:, it * HB + ti * 128: it * HB + (ti + 1) * 128],
                        w3T[:, it * HB + tch * 512: it * HB + (tch + 1) * 512],
                        start=(it == 0), stop=(it == 31))
                    if last:
                        v_o = TE.inc(fn, smm)
                    else:
                        TE.op(fn)
            hD_mm_last = v_o
            xt_mm_last = v_o
            slr = ti % 2
            SY.wait(sv, sv.n)
            xv = SY.dma(lambda e, tloc=tloc, slr=slr: e.dma_start(
                out=x3t[slr][:], in_=X3_re[:, tloc, :]), s_x3[slr])
            V.wait(smm, v_o)
            V.wait(s_x3[slr], xv)
            V.wait(s_x4, x4_store)
            V.op(lambda e, pj=pj: e.tensor_tensor(xo_t[:], pj[:], s3r[:], op=OP.mult))
            pinfo["free"] = sv.n
            v_fin = V.op(lambda e, slr=slr: e.tensor_tensor(xo_t[:], xo_t[:], x3t[slr][:], op=OP.add))
            SY.wait(sv, v_fin)
            x4_store = SY.dma(lambda e, tloc=tloc: e.dma_start(
                out=X4_re[:, tloc, :], in_=xo_t[:]), s_x4)
        gD_mm_last = smm.n

    SY.wait(s_x4, s_x4.n)
    return p.build()


def params_D(z):
    z = {k: np.asarray(v) for k, v in z.items() if k.startswith("L1_ffn")}
    return {
        "TW1": z["L1_ffn_w1_w"], "TW2": z["L1_ffn_w2_w"],
        "TH1": z["L1_ffn_h1_w"], "TH2": z["L1_ffn_h2_w"], "TW3": z["L1_ffn_w3_w"],
        "S1C": z["L1_ffn_w1_s"].reshape(32, 128).T.copy(),
        "S2C": z["L1_ffn_w2_s"].reshape(32, 128).T.copy(),
        "SH1C": z["L1_ffn_h1_s"].reshape(32, 128).T.copy(),
        "SH2C": z["L1_ffn_h2_s"].reshape(32, 128).T.copy(),
        "S3R": np.ascontiguousarray(np.broadcast_to(z["L1_ffn_w3_s"], (128, C))),
        "IDN": np.eye(128, dtype=BF),
    }


def _asm(full, res, key):
    out = np.zeros((2, T, C), np.float32)
    for core in range(8):
        b, r = core // 4, core % 4
        out[b, r * TC:(r + 1) * TC] = res.results[core][key]
    return out


def consts_A2(r):
    tl = np.arange(TC)
    lg = np.log(np.float64(GAMMA))
    if r == 3:
        gc = np.zeros(TC, np.float32)
    else:
        gc = np.exp((TC - tl) * lg).astype(np.float32)
    return gc.reshape(16, 128).T.copy()


def kernel(**z):
    x0 = np.asarray(z["L0_attn_X_Q"], np.float32)

    ins = []
    for core in range(8):
        b, r = core // 4, core % 4
        d = {"XR": np.ascontiguousarray(x0[b, r * TC:(r + 1) * TC]),
             "WQ": np.asarray(z["L0_attn_W_Q"]), "WK": np.asarray(z["L0_attn_W_K"]),
             "WV": np.asarray(z["L0_attn_W_V"])}
        d.update(consts_A(r))
        ins.append(d)
    resA = run_bass_kernel_spmd(build_A(), ins, core_ids=list(range(8)))

    gnw, gnb = np.asarray(z["L0_attn_gn_weight"]), np.asarray(z["L0_attn_gn_bias"])
    zero_s = np.zeros((8, C, C), BF)
    ins = []
    for core in range(8):
        b, r = core // 4, core % 4
        s_in = resA.results[core + 1]["S_LOC"] if r < 3 else zero_s
        ins.append({"OUTP": resA.results[core]["OUT_PART"], "S_IN": np.ascontiguousarray(s_in),
                    "KT": resA.results[core]["KT_SAVE"],
                    "XR": np.ascontiguousarray(x0[b, r * TC:(r + 1) * TC]),
                    "GNW": np.ascontiguousarray(np.broadcast_to(gnw, (128, C))),
                    "GNB": np.ascontiguousarray(np.broadcast_to(gnb, (128, C))),
                    "GCORR": consts_A2(r)})
    resA2 = run_bass_kernel_spmd(build_A2(), ins, core_ids=list(range(8)))
    x1 = _asm(None, resA2, "X1")

    ins = []
    for core in range(8):
        b, r = core // 4, core % 4
        d = {"YH": halo_rows(x1[b], core, HALO_B)}
        d.update(params_B(z, 0))
        d.update(consts_B(core))
        ins.append(d)
    resB1 = run_bass_kernel_spmd(build_B(True, False), ins, core_ids=list(range(8)))
    y1 = _asm(None, resB1, "YOUT")

    ins = []
    for core in range(8):
        b, r = core // 4, core % 4
        d = {"YH": halo_rows(y1[b], core, HALO_B),
             "X1R": np.ascontiguousarray(x1[b, r * TC:(r + 1) * TC])}
        d.update(params_B(z, 1))
        d.update(consts_B(core))
        ins.append(d)
    resB2 = run_bass_kernel_spmd(build_B(False, True), ins, core_ids=list(range(8)))
    x2 = _asm(None, resB2, "YOUT")

    ins = []
    for core in range(8):
        b, r = core // 4, core % 4
        d = {"X2H": halo_rows(x2[b], core, HALO_C)}
        d.update(params_C(z))
        d.update(consts_C(core))
        ins.append(d)
    resC = run_bass_kernel_spmd(build_C(), ins, core_ids=list(range(8)))
    x3 = _asm(None, resC, "X3")

    ins = []
    for core in range(8):
        b, r = core // 4, core % 4
        d = {"X3R": np.ascontiguousarray(x3[b, r * TC:(r + 1) * TC])}
        d.update(params_D(z))
        ins.append(d)
    resD = run_bass_kernel_spmd(build_D(), ins, core_ids=list(range(8)))
    return _asm(None, resD, "X4")
